# version 01
# baseline (speedup 1.0000x reference)
import math
import numpy as np
import ml_dtypes
import concourse.bass as bass
import concourse.mybir as mybir
from concourse.bass_utils import run_bass_kernel_spmd

F32 = mybir.dt.float32
BF16 = mybir.dt.bfloat16
AF = mybir.ActivationFunctionType
ALU = mybir.AluOpType
AX = mybir.AxisListType
bfdt = ml_dtypes.bfloat16

D = 2048
S = 2048
NH = 16
RS = 1.0 / math.sqrt(128.0)

QB_EVEN = [15, 14, 13, 12, 3, 2, 1, 0]
QB_ODD = [11, 10, 9, 8, 7, 6, 5, 4]
SLOT_L = [16, 15, 14, 13, 8, 7, 6, 5]
N_KB = [sum(1 for L in SLOT_L if L > kb) for kb in range(16)]
MSK_LO = [min(s for s, L in enumerate(SLOT_L) if L - 5 <= kb < L) for kb in range(16)]
MSK_N = [N_KB[kb] - MSK_LO[kb] for kb in range(16)]
MSK_OFF = np.cumsum([0] + MSK_N).tolist()
N_FOLD = 64
FOLD_COLS = 32


def hadamard(n):
    H = np.array([[1.0]], np.float32)
    while H.shape[0] < n:
        H = np.block([[H, H], [H, -H]])
    return np.ascontiguousarray(H.astype(np.float32))


class Rec:

    def __init__(self):
        self.streams = {e: [] for e in ("sync", "gpsimd", "tensor", "scalar", "vector")}

    def check(self):
        ptr = {e: 0 for e in self.streams}
        sems = {}
        progress = True
        while progress:
            progress = False
            for e, st in self.streams.items():
                while ptr[e] < len(st):
                    op, s, v = st[ptr[e]]
                    if op == "i":
                        sems[s] = sems.get(s, 0) + v
                        ptr[e] += 1
                        progress = True
                    elif sems.get(s, 0) >= v:
                        ptr[e] += 1
                        progress = True
                    else:
                        break
        stuck = {e: (ptr[e], len(st), self.streams[e][ptr[e]]) for e, st in self.streams.items()
                 if ptr[e] < len(st)}
        if stuck:
            raise RuntimeError(f"schedule deadlock: {stuck}")


class Sem:
    def __init__(self, nc, rec, name):
        self.h = nc.alloc_semaphore(name)
        self.name = name
        self.n = 0
        self.rec = rec

    def inc(self, ename, inst, k=1):
        inst.then_inc(self.h, k)
        self.n += k
        self.rec.streams[ename].append(("i", self.name, self.n))
        return inst

    def waitv(self, ename, eng, val):
        val = max(0, val)
        self.rec.streams[ename].append(("w", self.name, val))
        eng.wait_ge(self.h, val)

    def wait_all(self, ename, eng):
        self.waitv(ename, eng, self.n)


def _sem_inc_now(sem, ename, eng, k=1):
    eng.sem_inc(sem.h, k)
    sem.n += k
    sem.rec.streams[ename].append(("i", sem.name, sem.n))


def build_nc(debug=0):
    import contextlib
    nc = bass.Bass("TRN2", target_bir_lowering=False)
    rec = Rec()

    def dram_in(name, shape, dt=F32):
        return nc.dram_tensor(name, shape, dt, kind="ExternalInput").ap()

    x_full = dram_in("x_full", [S, D])
    x_own = dram_in("x_own", [1024, D])
    wqkvT = dram_in("wqkvT", [D, 3 * D])
    woutT = dram_in("woutT", [D, D])
    wupT = dram_in("wupT", [D, 8192])
    wgateT = dram_in("wgateT", [D, 8192])
    wdownT = dram_in("wdownT", [8192, D])
    bqkv_pc = dram_in("bqkv_pc", [128, 48])
    bout_pc = dram_in("bout_pc", [128, 16])
    suug_pc = dram_in("suug_pc", [128, 16])
    sudp_pc = dram_in("sudp_pc", [128, 64])
    wsup_pc = dram_in("wsup_pc", [128, 64])
    wsgate_pc = dram_in("wsgate_pc", [128, 64])
    wsdp_pc = dram_in("wsdp_pc", [128, 16])
    h128_in = dram_in("h128c", [128, 128], BF16)
    identf_in = dram_in("identf", [128, 128])
    identb_in = dram_in("identb", [128, 128], BF16)
    onescol_in = dram_in("onescol", [128, 1], BF16)
    negrow_in = dram_in("negrow", [1, 128])
    epsc_in = dram_in("epsc", [128, 1])
    masks_in = dram_in("maskst", [40 * 128, 128], BF16)

    out = nc.dram_tensor("out", [1024, D], F32, kind="ExternalOutput").ap()

    kd = "ExternalOutput" if debug else "Internal"
    scr_xnf = nc.dram_tensor("scr_xnf", [S, D], BF16, kind=kd).ap()
    scr_xno = nc.dram_tensor("scr_xno", [1024, D], BF16, kind=kd).ap()
    scr_attn = nc.dram_tensor("scr_attn", [1024, D], F32, kind=kd).ap()
    scr_md = nc.dram_tensor("scr_md", [N_FOLD, 128, 64 * FOLD_COLS], BF16, kind=kd).ap()
    dbg_hout = nc.dram_tensor("dbg_hout", [16 * 128, 1024], BF16, kind=kd).ap() if debug else None
    dbg_h = nc.dram_tensor("dbg_h", [64 * 128, 512], BF16, kind=kd).ap() if debug else None
    dbg_v1 = nc.dram_tensor("dbg_v1", [16 * 128, 512], BF16, kind=kd).ap() if debug else None
    dbg_v1b = nc.dram_tensor("dbg_v1b", [16 * 128, 512], BF16, kind=kd).ap() if debug else None
    dbg_hb = nc.dram_tensor("dbg_hb", [64 * 128, 512], BF16, kind=kd).ap() if debug else None

    sems = {}

    def mk(name):
        sems[name] = Sem(nc, rec, name)
        return sems[name]

    PB = mk("PB")
    mk("sDBG")
    RAW = mk("RAW")

    def raw_edge(ename, eng, inst):
        RAW.inc(ename, inst, 1)
        RAW.waitv(ename, eng, RAW.n)

    def phase_gate(blk, dma_waits=None):
        target = PB.n + 5

        def body(ename):
            def f(eng):
                for sem, val in (dma_waits or {}).get(ename, []):
                    sem.waitv(ename, eng, val)
                if ename in ("tensor", "scalar", "vector", "gpsimd"):
                    eng.drain()
                _sem_inc_now(PB, ename, eng)
                PB.waitv(ename, eng, target)
            return f
        blk.sync(body("sync"))
        blk.gpsimd(body("gpsimd"))
        blk.tensor(body("tensor"))
        blk.scalar(body("scalar"))
        blk.vector(body("vector"))
        assert PB.n == target

    ctx = contextlib.ExitStack()
    try:
        V_b = ctx.enter_context(nc.sbuf_tensor("V_b", [128, 305], F32, side="left"))
        C_BQ, C_BQS, C_BOUT, C_SUUG, C_SUDP, C_SUP, C_SGATE, C_SDP, C_EPS = 0, 48, 64, 80, 96, 160, 224, 288, 304
        H128s = ctx.enter_context(nc.sbuf_tensor("H128s", [128, 128], BF16, side="left"))
        IDB = ctx.enter_context(nc.sbuf_tensor("IDB", [128, 128], BF16, side="left"))
        IDF = ctx.enter_context(nc.sbuf_tensor("IDF", [128, 128], F32, side="left"))
        ONEC = ctx.enter_context(nc.sbuf_tensor("ONEC", [128, 1], BF16, side="left"))
        NEGR = ctx.enter_context(nc.sbuf_tensor("NEGR", [1, 128], F32, side="left"))
        DUMM = ctx.enter_context(nc.sbuf_tensor("DUMM", [128, 64], F32, side="left"))

        hout_ctx = contextlib.ExitStack()
        HOUT = hout_ctx.enter_context(nc.sbuf_tensor("HOUT", [128, 16 * 1024], BF16, side="left"))

        xnf_ctx = contextlib.ExitStack()
        xno = xnf_ctx.enter_context(nc.sbuf_tensor("xno", [128, 16 * 1024], BF16, side="right"))
        xnf = xnf_ctx.enter_context(nc.sbuf_tensor("xnf", [128, 16 * 2048], BF16, side="right"))

        sVLD = mk("sVLD")
        sBNS = mk("sBNS")
        sXT = mk("sXT")
        sSTAT = mk("sSTAT")
        sRSTD = mk("sRSTD")
        sXN = mk("sXN")
        sXNW = mk("sXNW")
        sXNT = mk("sXNT")
        NT = 24

        with (
            nc.sbuf_tensor("xstg", [128, 2 * 2048], F32, side="right") as xstg,
            nc.sbuf_tensor("xntg", [128, 2 * 2048], BF16, side="right") as xntg,
            nc.sbuf_tensor("bns", [128, 2 * 24], F32, side="right") as bns,
            nc.sbuf_tensor("mv", [128, 2 * 2], F32, side="right") as mv,
            nc.sbuf_tensor("rstd", [128, 2], F32, side="right") as rstd,
            nc.Block() as blk,
        ):
            @blk.sync
            def _(sy):
                e = "sync"
                for src, dst in ((h128_in, H128s), (identb_in, IDB), (identf_in, IDF),
                                 (onescol_in, ONEC), (negrow_in, NEGR)):
                    sVLD.inc(e, sy.dma_start(dst[:, :], src), 16)
                sVLD.inc(e, sy.dma_start(V_b[:, C_BQ:C_BQ + 48], bqkv_pc), 16)
                sVLD.inc(e, sy.dma_start(V_b[:, C_BOUT:C_BOUT + 16], bout_pc), 16)
                sVLD.inc(e, sy.dma_start(V_b[:, C_SUUG:C_SUUG + 16], suug_pc), 16)
                sVLD.inc(e, sy.dma_start(V_b[:, C_SUDP:C_SUDP + 64], sudp_pc), 16)
                sVLD.inc(e, sy.dma_start(V_b[:, C_SUP:C_SUP + 64], wsup_pc), 16)
                sVLD.inc(e, sy.dma_start(V_b[:, C_SGATE:C_SGATE + 64], wsgate_pc), 16)
                sVLD.inc(e, sy.dma_start(V_b[:, C_SDP:C_SDP + 16], wsdp_pc), 16)
                sVLD.inc(e, sy.dma_start(V_b[:, C_EPS:C_EPS + 1], epsc_in), 16)
                def wr(i):
                    sXN.waitv(e, sy, i + 1)
                    dstr = (scr_xnf[i * 128:(i + 1) * 128, :] if i < 16
                            else scr_xno[(i - 16) * 128:(i - 15) * 128, :])
                    sXNW.inc(e, sy.dma_start(dstr, xntg[:, (i % 2) * 2048:(i % 2 + 1) * 2048]), 16)

                for i in range(NT):
                    src = (x_full[i * 128:(i + 1) * 128, :] if i < 16
                           else x_own[(i - 16) * 128:(i - 15) * 128, :])
                    if i >= 2:
                        sXN.waitv(e, sy, i - 1)
                    sXT.inc(e, sy.dma_start(xstg[:, (i % 2) * 2048:(i % 2 + 1) * 2048], src), 16)
                    if i >= 1:
                        wr(i - 1)
                wr(NT - 1)
                sXNW.waitv(e, sy, 16 * 16)
                for c in range(16):
                    sXNT.inc(e, sy.dma_start_transpose(xnf[:, c * 2048:(c + 1) * 2048],
                                                       scr_xnf[:, c * 128:(c + 1) * 128]), 16)
                sXNW.waitv(e, sy, 16 * 24)
                for c in range(16):
                    sXNT.inc(e, sy.dma_start_transpose(xno[:, c * 1024:(c + 1) * 1024],
                                                       scr_xno[:, c * 128:(c + 1) * 128]), 16)

            @blk.vector
            def _(v):
                e = "vector"
                sVLD.waitv(e, v, 16 * 13)
                v.tensor_scalar(V_b[:, C_BQS:C_BQS + 16], V_b[:, C_BQ:C_BQ + 16], RS, None, ALU.mult)
                sBNS.inc(e, v.tensor_scalar(V_b[:, C_SUP:C_SUP + 64], V_b[:, C_SUP:C_SUP + 64],
                                            1.0 / math.sqrt(2048.0), None, ALU.mult), 1)
                sBNS.waitv(e, v, sBNS.n)
                v.tensor_tensor(V_b[:, C_SUP:C_SUP + 64], V_b[:, C_SUP:C_SUP + 64],
                                V_b[:, C_SUDP:C_SUDP + 64], ALU.mult)
                v.tensor_scalar(V_b[:, C_SGATE:C_SGATE + 64], V_b[:, C_SGATE:C_SGATE + 64],
                                1.0 / math.sqrt(2048.0), None, ALU.mult)
                v.tensor_scalar(V_b[:, C_SDP:C_SDP + 16], V_b[:, C_SDP:C_SDP + 16],
                                1.0 / math.sqrt(8192.0), None, ALU.mult)
                for i in range(NT):
                    p = i % 2
                    sXT.waitv(e, v, 16 * (i + 1))
                    if i >= 2:
                        sXNW.waitv(e, v, 16 * (i - 1))
                        sRSTD.waitv(e, v, i - 1)
                    for q in range(4):
                        sBNS.inc(e, v.bn_stats(bns[:, p * 24 + q * 6:p * 24 + (q + 1) * 6],
                                               xstg[:, p * 2048 + q * 512:p * 2048 + (q + 1) * 512]), 1)
                    sBNS.waitv(e, v, 1 + 4 * (i + 1))
                    sSTAT.inc(e, v.bn_aggr(mv[:, p * 2:(p + 1) * 2], bns[:, p * 24:(p + 1) * 24]), 1)
                    sRSTD.waitv(e, v, i + 1)
                    sXN.inc(e, v.tensor_scalar(xntg[:, p * 2048:(p + 1) * 2048],
                                               xstg[:, p * 2048:(p + 1) * 2048],
                                               mv[:, p * 2:p * 2 + 1], rstd[:, p:p + 1],
                                               ALU.subtract, ALU.mult), 1)

            @blk.scalar
            def _(s):
                e = "scalar"
                for i in range(NT):
                    p = i % 2
                    sSTAT.waitv(e, s, i + 1)
                    if i >= 2:
                        sXN.waitv(e, s, i - 1)
                    sVLD.waitv(e, s, 16 * 13)
                    raw_edge(e, s, s.activation(rstd[:, p:p + 1], mv[:, p * 2 + 1:p * 2 + 2], AF.Ln,
                                                bias=V_b[:, C_EPS:C_EPS + 1]))
                    sRSTD.inc(e, s.activation(rstd[:, p:p + 1], rstd[:, p:p + 1], AF.Exp, scale=-0.5), 1)

        with nc.Block() as blk:
            phase_gate(blk, dma_waits={"sync": [(sXNT, 16 * 32)]})

        sWQ = mk("sWQ")
        sMSK = mk("sMSK")
        sPEQ = mk("sPEQ")
        sSEV = mk("sSEV")
        sTMM = mk("sTMM")
        sTEV = mk("sTEV")
        sSTM = mk("sSTM")
        sEXP = mk("sEXP")
        sPMK = mk("sPMK")
        sAVD = mk("sAVD")
        sRCP = mk("sRCP")
        sBCT = mk("sBCT")
        sBEX = mk("sBEX")
        sHOE = mk("sHOE")
        sFIN = mk("sFIN")
        sFLD = mk("sFLD")
        sFOD = mk("sFOD")

        with (
            nc.sbuf_tensor("wq", [128, 3 * 16 * 128], BF16, side="right") as wq,
            nc.sbuf_tensor("KT", [128, 2 * 2048], BF16, side="right") as KT,
            nc.sbuf_tensor("VTf", [128, 2048], BF16, side="right") as VTf,
            nc.sbuf_tensor("Vtm", [128, 2 * 2048], BF16, side="right") as Vtm,
            nc.sbuf_tensor("QT", [128, 2 * 1024], BF16, side="right") as QT,
            nc.sbuf_tensor("MASKS", [128, 40 * 128], BF16, side="right") as MASKS,
            nc.sbuf_tensor("PT", [128, 2 * 1024], BF16, side="right") as PT,
            nc.sbuf_tensor("BREC", [128, 1024], F32, side="right") as BREC,
            nc.sbuf_tensor("LNL", [1, 1024], F32, side="right") as LNL,
            nc.sbuf_tensor("FA", [128, 64 * FOLD_COLS], F32, side="right") as FA,
            nc.sbuf_tensor("FB", [128, 64 * FOLD_COLS], F32, side="right") as FB,
            nc.psum_tensor("pST", [128, 1024], F32) as pST,
            nc.psum_tensor("pHO", [128, 1024], F32) as pHO,
            nc.psum_tensor("pL", [1, 1024], F32) as pL,
            nc.psum_tensor("pQKV", [128, 512], F32) as pQKV,
            nc.psum_tensor("pVT", [128, 1024], BF16) as pVT,
            nc.Block() as blk,
        ):
            def fold_piece(eng, ename):
                bufs = [FA, FB]
                ins = None
                for lvl in range(6):
                    src, dst = bufs[lvl % 2], bufs[(lvl + 1) % 2]
                    half = 1 << lvl
                    sv = src[:, :].rearrange("p (a two b c) -> p a two b c",
                                             two=2, b=half, c=FOLD_COLS)
                    dv = dst[:, :].rearrange("p (a two b c) -> p a two b c",
                                             two=2, b=half, c=FOLD_COLS)
                    eng.tensor_tensor(dv[:, :, 0, :, :], sv[:, :, 0, :, :], sv[:, :, 1, :, :], ALU.add)
                    ins = eng.tensor_tensor(dv[:, :, 1, :, :], sv[:, :, 0, :, :], sv[:, :, 1, :, :], ALU.subtract)
                    if lvl < 5:
                        raw_edge(ename, eng, ins)
                sFLD.inc(ename, ins, 1)

            @blk.gpsimd
            def _(g):
                e = "gpsimd"
                PB.waitv(e, g, PB.n)
                fold_out = [0]

                def emit_fold_outs(upto):
                    while fold_out[0] < min(upto, N_FOLD):
                        i = fold_out[0]
                        sFLD.waitv(e, g, i + 1)
                        sFOD.inc(e, g.dma_start(scr_md[i, :, :], FA[:, :]), 16)
                        fold_out[0] += 1

                for h in range(16):
                    if h >= 1:
                        sPEQ.waitv(e, g, 10 * h)
                    for j in range(3):
                        ins = g.dma_start(
                            wq[:, :].rearrange("p (j kc m) -> p j kc m", j=3, m=128)[:, j, :, :],
                            wqkvT[:, :].rearrange("(kc p) (j hh m) -> p j hh kc m", p=128, j=3, m=128)[:, j, h, :, :])
                        sWQ.inc(e, ins, 16)
                    emit_fold_outs(4 * (h - 1))
                emit_fold_outs(N_FOLD)

            @blk.sync
            def _(sy):
                e = "sync"
                PB.waitv(e, sy, PB.n)
                sMSK.inc(e, sy.dma_start(MASKS[:, :].rearrange("p (t m) -> p t m", m=128),
                                         masks_in.rearrange("(t p) m -> p t m", p=128)), 16)
                for i in range(N_FOLD):
                    if i >= 1:
                        sFOD.waitv(e, sy, 16 * i)
                    o, qq = i // 4, i % 4
                    src = wdownT[:, :].rearrange("(c p) o -> p c o", p=128)[
                        :, :, o * 128 + qq * FOLD_COLS: o * 128 + (qq + 1) * FOLD_COLS]
                    sFIN.inc(e, sy.dma_start(FA[:, :].rearrange("p (c m) -> p c m", m=FOLD_COLS), src), 16)

            @blk.tensor
            def _(t):
                e = "tensor"
                PB.waitv(e, t, PB.n)

                def qkv_group(h, g):
                    sWQ.waitv(e, t, 16 * 3 * (h + 1))
                    sSEV.waitv(e, t, 10 * h + g)
                    wcol = (0 if g < 2 else (16 * 128 if g < 6 else 32 * 128))
                    tg = g if g < 2 else (g - 2 if g < 6 else g - 6)
                    ins = None
                    for kc in range(16):
                        lhs = wq[:, wcol + kc * 128:wcol + (kc + 1) * 128]
                        if g >= 2:
                            rhs = xnf[:, kc * 2048 + tg * 512: kc * 2048 + (tg + 1) * 512]
                        else:
                            rhs = xno[:, kc * 1024 + tg * 512: kc * 1024 + (tg + 1) * 512]
                        ins = t.matmul(pQKV[:, :], lhs, rhs, start=(kc == 0), stop=(kc == 15))
                    sPEQ.inc(e, ins, 1)

                def v_transposes(h, b):
                    sSEV.waitv(e, t, 10 * h + 8 + 2 * b)
                    sTEV.waitv(e, t, 2 * h + b)
                    ins = None
                    for tt in range(8):
                        ins = t.transpose(pVT[:, tt * 128:(tt + 1) * 128],
                                          VTf[:, (b * 8 + tt) * 128:(b * 8 + tt + 1) * 128], IDB[:, :])
                    sTMM.inc(e, ins, 1)

                def do_avl(h, kb):
                    par = h % 2
                    n = N_KB[kb]
                    w = n * 128
                    sPMK.waitv(e, t, 16 * h + kb + 1)
                    if kb == 0:
                        sHOE.waitv(e, t, 2 * h)
                        sRCP.waitv(e, t, h)
                    pcol = (kb % 2) * 1024
                    t.matmul(pHO[:, 0:min(w, 512)],
                             Vtm[:, par * 2048 + kb * 128: par * 2048 + (kb + 1) * 128],
                             PT[:, pcol:pcol + min(w, 512)], start=(kb == 0), stop=(kb == 15))
                    ins = t.matmul(pL[:, 0:min(w, 512)], ONEC[:, :], PT[:, pcol:pcol + min(w, 512)],
                                    start=(kb == 0), stop=(kb == 15))
                    if w > 512:
                        t.matmul(pHO[:, 512:w],
                                 Vtm[:, par * 2048 + kb * 128: par * 2048 + (kb + 1) * 128],
                                 PT[:, pcol + 512:pcol + w], start=(kb == 0), stop=(kb == 7))
                        ins = t.matmul(pL[:, 512:w], ONEC[:, :], PT[:, pcol + 512:pcol + w],
                                       start=(kb == 0), stop=(kb == 7))
                    sAVD.inc(e, ins, 1)

                for g in range(10):
                    qkv_group(0, g)
                for b in range(2):
                    v_transposes(0, b)

                for h in range(16):
                    par = h % 2
                    for kb in range(16):
                        n = N_KB[kb]
                        w = n * 128
                        if h + 1 < 16:
                            if kb < 10:
                                qkv_group(h + 1, kb)
                            elif kb in (10, 11):
                                v_transposes(h + 1, kb - 10)
                        if kb == 0:
                            sSEV.waitv(e, t, 10 * h + 10)
                            sTEV.waitv(e, t, 2 * h + 2)
                            sBEX.waitv(e, t, h)
                        else:
                            sEXP.waitv(e, t, 16 * h + kb)
                        lhsk = KT[:, par * 2048 + kb * 128: par * 2048 + (kb + 1) * 128]
                        ins = t.matmul(pST[:, 0:min(w, 512)], lhsk,
                                       QT[:, par * 1024: par * 1024 + min(w, 512)], start=True, stop=True)
                        if w > 512:
                            ins = t.matmul(pST[:, 512:w], lhsk,
                                           QT[:, par * 1024 + 512: par * 1024 + w], start=True, stop=True)
                        sSTM.inc(e, ins, 1)
                        if kb >= 1:
                            do_avl(h, kb - 1)
                    do_avl(h, 15)
                    sRCP.waitv(e, t, h + 1)
                    sEXP.waitv(e, t, 16 * h + 16)
                    t.matmul(pST[:, 0:512], NEGR[:, :], LNL[:, 0:512], start=True, stop=True)
                    ins = t.matmul(pST[:, 512:1024], NEGR[:, :], LNL[:, 512:1024], start=True, stop=True)
                    sBCT.inc(e, ins, 1)

            @blk.scalar
            def _(s):
                e = "scalar"
                PB.waitv(e, s, PB.n)

                def qkv_evac(h, g):
                    par = h % 2
                    tg = g if g < 2 else (g - 2 if g < 6 else g - 6)
                    sPEQ.waitv(e, s, 10 * h + g + 1)
                    if g == 6:
                        sTMM.waitv(e, s, 2 * h)
                    if g < 2:
                        ins = s.activation(QT[:, par * 1024 + tg * 512: par * 1024 + (tg + 1) * 512],
                                           pQKV[:, :], AF.Identity, bias=V_b[:, C_BQS + h:C_BQS + h + 1], scale=RS)
                    elif g < 6:
                        ins = s.activation(KT[:, par * 2048 + tg * 512: par * 2048 + (tg + 1) * 512],
                                           pQKV[:, :], AF.Identity, bias=V_b[:, C_BQ + 16 + h:C_BQ + 17 + h])
                    else:
                        ins = s.activation(VTf[:, tg * 512:(tg + 1) * 512],
                                           pQKV[:, :], AF.Identity, bias=V_b[:, C_BQ + 32 + h:C_BQ + 33 + h])
                    sSEV.inc(e, ins, 1)

                def v_tevac(h, b):
                    par = h % 2
                    sTMM.waitv(e, s, 2 * h + b + 1)
                    ins = None
                    for tt in range(8):
                        ins = s.copy(Vtm[:, par * 2048 + (b * 8 + tt) * 128: par * 2048 + (b * 8 + tt + 1) * 128],
                                     pVT[:, tt * 128:(tt + 1) * 128])
                    sTEV.inc(e, ins, 1)

                for g in range(10):
                    qkv_evac(0, g)
                for b in range(2):
                    v_tevac(0, b)

                for h in range(16):
                    for kb in range(16):
                        if h + 1 < 16:
                            if kb < 10:
                                qkv_evac(h + 1, kb)
                            elif kb in (10, 11):
                                v_tevac(h + 1, kb - 10)
                        n = N_KB[kb]
                        sSTM.waitv(e, s, 16 * h + kb + 1)
                        sAVD.waitv(e, s, 16 * h + kb - 1)
                        ins = s.activation(PT[:, (kb % 2) * 1024:(kb % 2) * 1024 + n * 128],
                                           pST[:, 0:n * 128], AF.Exp)
                        sEXP.inc(e, ins, 1)
                    sAVD.waitv(e, s, 16 * h + 16)
                    sBCT.waitv(e, s, h)
                    sRCP.inc(e, s.activation(LNL[:, :], pL[:, :], AF.Ln), 1)
                    sBCT.waitv(e, s, h + 1)
                    sHOE.waitv(e, s, 2 * h)
                    sBEX.inc(e, s.activation(BREC[:, :], pST[:, 0:1024], AF.Exp), 1)

            @blk.vector
            def _(v):
                e = "vector"
                PB.waitv(e, v, PB.n)
                sMSK.waitv(e, v, 16)
                fold_next = [0]

                def fold_some(upto):
                    while fold_next[0] < min(upto, N_FOLD):
                        i = fold_next[0]
                        sFIN.waitv(e, v, 16 * (i + 1))
                        fold_piece(v, e)
                        fold_next[0] += 1

                for h in range(16):
                    for kb in range(16):
                        n, lo = N_KB[kb], MSK_LO[kb]
                        pcol = (kb % 2) * 1024
                        sEXP.waitv(e, v, 16 * h + kb + 1)
                        ins = v.tensor_tensor(PT[:, pcol + lo * 128:pcol + n * 128],
                                              PT[:, pcol + lo * 128:pcol + n * 128],
                                              MASKS[:, MSK_OFF[kb] * 128:MSK_OFF[kb + 1] * 128], ALU.mult)
                        sPMK.inc(e, ins, 1)
                    sBEX.waitv(e, v, h + 1)
                    sAVD.waitv(e, v, 16 * h + 16)
                    for half in range(2):
                        ins = v.tensor_tensor(HOUT[:, h * 1024 + half * 512: h * 1024 + (half + 1) * 512],
                                              pHO[:, half * 512:(half + 1) * 512],
                                              BREC[:, half * 512:(half + 1) * 512], ALU.mult)
                        sHOE.inc(e, ins, 1)
                    fold_some(4 * h)
                fold_some(N_FOLD)

            assert sPEQ.n == 160 and sSEV.n == 160 and sEXP.n == 256 and sAVD.n == 256
            assert sFLD.n == N_FOLD and sHOE.n == 32

        with nc.Block() as blk:
            phase_gate(blk, dma_waits={"gpsimd": [(sFOD, 16 * N_FOLD)], "sync": [(sFIN, 16 * N_FOLD)]})
        xnf_ctx.close()

        sWO = mk("sWO")
        sOPM = mk("sOPM")
        sOPE = mk("sOPE")
        sOTR = mk("sOTR")
        sOTC = mk("sOTC")
        sATW = mk("sATW")

        with (
            nc.sbuf_tensor("wo", [128, 2 * 16 * 128], BF16, side="right") as wo,
            nc.sbuf_tensor("SF", [128, 2 * 512], F32, side="right") as SF,
            nc.sbuf_tensor("ATM", [128, 4 * 2048], F32, side="right") as ATM,
            nc.psum_tensor("pOP", [128, 2 * 512], F32) as pOP,
            nc.psum_tensor("pTR", [128, 2 * 512], F32) as pTR,
            nc.Block() as blk,
        ):
            def idx(tg, oc):
                return tg * 16 + oc

            @blk.gpsimd
            def _(g):
                e = "gpsimd"
                PB.waitv(e, g, PB.n)
                for tg in range(2):
                    for oc in range(16):
                        i = idx(tg, oc)
                        sOPM.waitv(e, g, i - 1)
                        ins = g.dma_start(
                            wo[:, (i % 2) * 2048:(i % 2 + 1) * 2048].rearrange("p (kc m) -> p kc m", m=128),
                            woutT[:, :].rearrange("(kc p) m -> p kc m", p=128)[:, :, oc * 128:(oc + 1) * 128])
                        sWO.inc(e, ins, 16)

            def do_tr(t, e, i):
                sOPE.waitv(e, t, i + 1)
                sOTC.waitv(e, t, 4 * (i - 1))
                ins = None
                for tt in range(4):
                    ins = t.transpose(pTR[:, (i % 2) * 512 + tt * 128:(i % 2) * 512 + (tt + 1) * 128],
                                      SF[:, (i % 2) * 512 + tt * 128:(i % 2) * 512 + (tt + 1) * 128], IDF[:, :])
                sOTR.inc(e, ins, 1)

            @blk.tensor
            def _(t):
                e = "tensor"
                PB.waitv(e, t, PB.n)
                for tg in range(2):
                    for oc in range(16):
                        i = idx(tg, oc)
                        sWO.waitv(e, t, 16 * (i + 1))
                        sOPE.waitv(e, t, i - 1)
                        ins = None
                        for hc in range(16):
                            ins = t.matmul(pOP[:, (i % 2) * 512:(i % 2 + 1) * 512],
                                           wo[:, (i % 2) * 2048 + hc * 128:(i % 2) * 2048 + (hc + 1) * 128],
                                           HOUT[:, hc * 1024 + tg * 512: hc * 1024 + (tg + 1) * 512],
                                           start=(hc == 0), stop=(hc == 15))
                        sOPM.inc(e, ins, 1)
                        if i >= 1:
                            do_tr(t, e, i - 1)
                do_tr(t, e, 31)

            @blk.scalar
            def _(s):
                e = "scalar"
                PB.waitv(e, s, PB.n)
                for tg in range(2):
                    for oc in range(16):
                        i = idx(tg, oc)
                        sOPM.waitv(e, s, i + 1)
                        if i >= 2:
                            sOTR.waitv(e, s, i - 1)
                        ins = s.activation(SF[:, (i % 2) * 512:(i % 2 + 1) * 512],
                                           pOP[:, (i % 2) * 512:(i % 2 + 1) * 512],
                                           AF.Identity, bias=V_b[:, C_BOUT + oc:C_BOUT + oc + 1])
                        sOPE.inc(e, ins, 1)

            @blk.vector
            def _(v):
                e = "vector"
                PB.waitv(e, v, PB.n)
                for tg in range(2):
                    for oc in range(16):
                        i = idx(tg, oc)
                        oc_ = i % 16
                        sOTR.waitv(e, v, i + 1)
                        if tg == 1 and oc == 0:
                            sATW.waitv(e, v, 16 * 4)
                        ins = None
                        for tt in range(4):
                            ins = v.tensor_copy(ATM[:, tt * 2048 + oc_ * 128: tt * 2048 + (oc_ + 1) * 128],
                                                pTR[:, (i % 2) * 512 + tt * 128:(i % 2) * 512 + (tt + 1) * 128])
                        sOTC.inc(e, ins, 4)

            @blk.sync
            def _(sy):
                e = "sync"
                PB.waitv(e, sy, PB.n)
                if debug:
                    for hh_ in range(16):
                        sems["sDBG"].inc(e, sy.dma_start(dbg_hout[hh_ * 128:(hh_ + 1) * 128, :],
                                                         HOUT[:, hh_ * 1024:(hh_ + 1) * 1024]), 16)
                for tg in range(2):
                    sOTC.waitv(e, sy, (tg + 1) * 64)
                    for tt in range(4):
                        sATW.inc(e, sy.dma_start(scr_attn[tg * 512 + tt * 128: tg * 512 + (tt + 1) * 128, :],
                                                 ATM[:, tt * 2048:(tt + 1) * 2048]), 16)

        with nc.Block() as blk:
            phase_gate(blk, dma_waits={"sync": [(sATW, 16 * 8)], "gpsimd": [(sWO, 16 * 32)]})
        hout_ctx.close()

        sXNH = mk("sXNH")
        sACX = mk("sACX")
        sACA = mk("sACA")
        sV1B = mk("sV1B")
        sSA1 = mk("sSA1")
        sSA1E = mk("sSA1E")
        sUGD = mk("sUGD")
        sUPM = mk("sUPM")
        sSIL = mk("sSIL")
        sHMU = mk("sHMU")
        sSA2 = mk("sSA2")
        sSA2E = mk("sSA2E")
        sMDL = mk("sMDL")
        sMDR = mk("sMDR")
        sDNM = mk("sDNM")
        sDNE = mk("sDNE")
        sDTR = mk("sDTR")
        sACC = mk("sACC")
        sOUT = mk("sOUT")

        with (
            nc.sbuf_tensor("XNOH", [128, 16 * 512], BF16, side="right") as XNOH,
            nc.sbuf_tensor("ACC", [128, 4 * 2048], F32, side="right") as ACC,
            nc.sbuf_tensor("HBUF", [128, 64 * 512], BF16, side="right") as HBUF,
            nc.sbuf_tensor("V1", [128, 16 * 512], BF16, side="right") as V1,
            nc.sbuf_tensor("WppA", [128, 1024], F32, side="right") as WppA,
            nc.sbuf_tensor("WppB", [128, 1024], F32, side="right") as WppB,
            nc.sbuf_tensor("WU", [128, 2 * 2048], BF16, side="right") as WU,
            nc.sbuf_tensor("WG", [128, 2 * 2048], BF16, side="right") as WG,
            nc.sbuf_tensor("SG", [128, 2 * 512], F32, side="right") as SG,
            nc.sbuf_tensor("MDRAW", [128, 4096], BF16, side="right") as MDRAW,
            nc.sbuf_tensor("MD", [128, 2 * 8192], BF16, side="right") as MD,
            nc.sbuf_tensor("DF", [128, 2 * 512], F32, side="right") as DF,
            nc.psum_tensor("pSA", [128, 2 * 512], F32) as pSA,
            nc.psum_tensor("pUP", [128, 2 * 512], F32) as pUP,
            nc.psum_tensor("pGA", [128, 2 * 512], F32) as pGA,
            nc.psum_tensor("pDN", [128, 2 * 512], F32) as pDN,
            nc.Block() as blk,
        ):
            PBASE = PB.n

            def J(t2, j):
                return t2 * 64 + j

            def O(t2, o):
                return t2 * 16 + o

            @blk.sync
            def _(sy):
                e = "sync"
                PB.waitv(e, sy, PBASE)
                for t2 in range(2):
                    if debug and t2 == 1:
                        pass
                    if t2 == 1:
                        sV1B.waitv(e, sy, 8)
                    for c in range(16):
                        sXNH.inc(e, sy.dma_start_transpose(
                            XNOH[:, c * 512:(c + 1) * 512],
                            scr_xno[t2 * 512:(t2 + 1) * 512, c * 128:(c + 1) * 128]), 16)
                    if debug and t2 == 0:
                        sSA1E.waitv(e, sy, 16)
                        for cc_ in range(16):
                            sems["sDBG"].inc(e, sy.dma_start(dbg_v1[cc_ * 128:(cc_ + 1) * 128, :],
                                                             V1[:, cc_ * 512:(cc_ + 1) * 512]), 16)
                    for tt in range(4):
                        if t2 == 1:
                            sOUT.waitv(e, sy, 16 * 4)
                        sACX.inc(e, sy.dma_start(ACC[:, tt * 2048:(tt + 1) * 2048],
                                                 x_own[t2 * 512 + tt * 128: t2 * 512 + (tt + 1) * 128, :]), 16)
                    for o in range(16):
                        for hh in range(2):
                            i2 = O(t2, o) * 2 + hh
                            if i2 >= 1:
                                sMDR.waitv(e, sy, i2)
                            for qq in range(2):
                                sMDL.inc(e, sy.dma_start(MDRAW[:, qq * 2048:(qq + 1) * 2048],
                                                         scr_md[4 * o + hh * 2 + qq, :, :]), 16)
                    if debug:
                        sSA2E.waitv(e, sy, 64 * (t2 + 1))
                        dh = dbg_h if t2 == 0 else dbg_hb
                        for jj_ in range(64):
                            sems["sDBG"].inc(e, sy.dma_start(dh[jj_ * 128:(jj_ + 1) * 128, :],
                                                             HBUF[:, jj_ * 512:(jj_ + 1) * 512]), 16)
                        if t2 == 1:
                            sSA1E.waitv(e, sy, 32)
                            for cc_ in range(16):
                                sems["sDBG"].inc(e, sy.dma_start(dbg_v1b[cc_ * 128:(cc_ + 1) * 128, :],
                                                                 V1[:, cc_ * 512:(cc_ + 1) * 512]), 16)
                    for tt in range(4):
                        sACC.waitv(e, sy, 16 * (t2 + 1))
                        sOUT.inc(e, sy.dma_start(out[t2 * 512 + tt * 128: t2 * 512 + (tt + 1) * 128, :],
                                                 ACC[:, tt * 2048:(tt + 1) * 2048]), 16)
                if debug:
                    sems["sDBG"].wait_all(e, sy)
                sOUT.waitv(e, sy, 16 * 8)

            @blk.gpsimd
            def _(g):
                e = "gpsimd"
                PB.waitv(e, g, PBASE)
                for t2 in range(2):
                    for tt in range(4):
                        sACX.waitv(e, g, 16 * (t2 * 4 + tt + 1))
                        sACA.inc(e, g.dma_start(ACC[:, tt * 2048:(tt + 1) * 2048],
                                                scr_attn[t2 * 512 + tt * 128: t2 * 512 + (tt + 1) * 128, :],
                                                accum_op=ALU.add), 16)
                    for j in range(64):
                        i = J(t2, j)
                        sHMU.waitv(e, g, i - 1)
                        sUGD.inc(e, g.dma_start(WU[:, (j % 2) * 2048:(j % 2 + 1) * 2048].rearrange("p (kc m) -> p kc m", m=128),
                                                 wupT[:, :].rearrange("(kc p) m -> p kc m", p=128)[:, :, j * 128:(j + 1) * 128]), 16)
                        sUGD.inc(e, g.dma_start(WG[:, (j % 2) * 2048:(j % 2 + 1) * 2048].rearrange("p (kc m) -> p kc m", m=128),
                                                 wgateT[:, :].rearrange("(kc p) m -> p kc m", p=128)[:, :, j * 128:(j + 1) * 128]), 16)
                    for o in range(16):
                        i = O(t2, o)
                        for hh in range(2):
                            i2 = i * 2 + hh
                            sMDL.waitv(e, g, 32 * (i2 + 1))
                            if hh == 0 and i >= 2:
                                sDNM.waitv(e, g, i - 1)
                            srcv = MDRAW[:, :].rearrange("p (qq c m) -> p c qq m", qq=2, m=FOLD_COLS)
                            dstv = MD[:, (i % 2) * 8192:(i % 2 + 1) * 8192].rearrange(
                                "p (c qq2 m) -> p c qq2 m", qq2=4, m=FOLD_COLS)[:, :, hh * 2:(hh + 1) * 2, :]
                            sMDR.inc(e, g.tensor_copy(dstv, srcv), 1)

            @blk.vector
            def _(v):
                e = "vector"
                PB.waitv(e, v, PBASE)
                for t2 in range(2):
                    for pc in range(8):
                        if pc == 0:
                            sXNH.waitv(e, v, 16 * 16 * (t2 + 1))
                        if t2 == 1 and pc == 0:
                            sUPM.waitv(e, v, 64)
                        ins = None
                        for c in range(16):
                            ins = v.tensor_scalar(WppA[:, c * 64:(c + 1) * 64],
                                                  XNOH[:, c * 512 + pc * 64: c * 512 + (pc + 1) * 64],
                                                  V_b[:, C_SUUG + c:C_SUUG + c + 1], None, ALU.mult)
                        raw_edge(e, v, ins)
                        bufs = [WppA, WppB]
                        for lvl in range(4):
                            srcb, dstb = bufs[lvl % 2], bufs[(lvl + 1) % 2]
                            half = 1 << lvl
                            sv = srcb[:, :].rearrange("p (a two b c) -> p a two b c", two=2, b=half, c=64)
                            if lvl < 3:
                                dv = dstb[:, :].rearrange("p (a two b c) -> p a two b c", two=2, b=half, c=64)
                                v.tensor_tensor(dv[:, :, 0, :, :], sv[:, :, 0, :, :], sv[:, :, 1, :, :], ALU.add)
                                ins = v.tensor_tensor(dv[:, :, 1, :, :], sv[:, :, 0, :, :], sv[:, :, 1, :, :], ALU.subtract)
                                raw_edge(e, v, ins)
                            else:
                                dv = V1[:, :].rearrange("p (a two b c) -> p a two b c", two=2, b=half, c=512)[
                                    :, :, :, :, pc * 64:(pc + 1) * 64]
                                v.tensor_tensor(dv[:, :, 0, :, :], sv[:, :, 0, :, :], sv[:, :, 1, :, :], ALU.add)
                                ins = v.tensor_tensor(dv[:, :, 1, :, :], sv[:, :, 0, :, :], sv[:, :, 1, :, :], ALU.subtract)
                        if pc < 7:
                            sV1B.inc(e, ins, 1)
                        else:
                            sV1B.inc(e, v.tensor_copy(WppB[:, 0:512], WppA[:, 0:512]), 1)
                    for j in range(64):
                        i = J(t2, j)
                        sSIL.waitv(e, v, i + 1)
                        ins = v.scalar_tensor_tensor(HBUF[:, j * 512:(j + 1) * 512],
                                                     pUP[:, (j % 2) * 512:(j % 2 + 1) * 512],
                                                     V_b[:, C_SUP + j:C_SUP + j + 1],
                                                     SG[:, (j % 2) * 512:(j % 2 + 1) * 512],
                                                     ALU.mult, ALU.mult)
                        sHMU.inc(e, ins, 1)
                    for o in range(16):
                        i = O(t2, o)
                        sDTR.waitv(e, v, i + 1)
                        if o == 0:
                            sACA.waitv(e, v, 16 * 4 * (t2 + 1))
                        ins = None
                        for tt in range(4):
                            ins = v.tensor_tensor(ACC[:, tt * 2048 + o * 128: tt * 2048 + (o + 1) * 128],
                                                  ACC[:, tt * 2048 + o * 128: tt * 2048 + (o + 1) * 128],
                                                  pSA[:, (i % 2) * 512 + tt * 128:(i % 2) * 512 + (tt + 1) * 128],
                                                  ALU.add)
                        sACC.inc(e, ins, 1)

            def down_tr(t, e, i):
                sDNE.waitv(e, t, i + 1)
                sACC.waitv(e, t, i - 1)
                ins = None
                for tt in range(4):
                    ins = t.transpose(pSA[:, (i % 2) * 512 + tt * 128:(i % 2) * 512 + (tt + 1) * 128],
                                      DF[:, (i % 2) * 512 + tt * 128:(i % 2) * 512 + (tt + 1) * 128], IDF[:, :])
                sDTR.inc(e, ins, 1)

            @blk.tensor
            def _(t):
                e = "tensor"
                PB.waitv(e, t, PBASE)
                for t2 in range(2):
                    for c in range(16):
                        i = t2 * 16 + c
                        sV1B.waitv(e, t, 8 * (t2 + 1))
                        if t2 == 1 and c == 0:
                            sACC.waitv(e, t, 16)
                        sSA1E.waitv(e, t, i - 1)
                        ins = t.matmul(pSA[:, (c % 2) * 512:(c % 2 + 1) * 512], H128s[:, :],
                                       V1[:, c * 512:(c + 1) * 512], start=True, stop=True)
                        sSA1.inc(e, ins, 1)
                    for j in range(64):
                        i = J(t2, j)
                        sUGD.waitv(e, t, 32 * (i + 1))
                        sSA1E.waitv(e, t, 16 * (t2 + 1))
                        sHMU.waitv(e, t, i - 1)
                        sSIL.waitv(e, t, i - 1)
                        for kc in range(16):
                            t.matmul(pUP[:, (j % 2) * 512:(j % 2 + 1) * 512],
                                     WU[:, (j % 2) * 2048 + kc * 128:(j % 2) * 2048 + (kc + 1) * 128],
                                     V1[:, kc * 512:(kc + 1) * 512], start=(kc == 0), stop=(kc == 15))
                        ins = None
                        for kc in range(16):
                            ins = t.matmul(pGA[:, (j % 2) * 512:(j % 2 + 1) * 512],
                                           WG[:, (j % 2) * 2048 + kc * 128:(j % 2) * 2048 + (kc + 1) * 128],
                                           V1[:, kc * 512:(kc + 1) * 512], start=(kc == 0), stop=(kc == 15))
                        sUPM.inc(e, ins, 1)
                    for j in range(64):
                        i = J(t2, j)
                        sHMU.waitv(e, t, i + 1)
                        sSA2E.waitv(e, t, i - 1)
                        ins = t.matmul(pSA[:, (j % 2) * 512:(j % 2 + 1) * 512], H128s[:, :],
                                       HBUF[:, j * 512:(j + 1) * 512], start=True, stop=True)
                        sSA2.inc(e, ins, 1)
                    for o in range(16):
                        i = O(t2, o)
                        sMDR.waitv(e, t, 2 * (i + 1))
                        sSA2E.waitv(e, t, 64 * (t2 + 1))
                        sDNE.waitv(e, t, i - 1)
                        ins = None
                        for c in range(64):
                            ins = t.matmul(pDN[:, (i % 2) * 512:(i % 2 + 1) * 512],
                                           MD[:, (i % 2) * 8192 + c * 128:(i % 2) * 8192 + (c + 1) * 128],
                                           HBUF[:, c * 512:(c + 1) * 512], start=(c == 0), stop=(c == 63))
                        sDNM.inc(e, ins, 1)
                        if o >= 2:
                            down_tr(t, e, i - 2)
                    down_tr(t, e, O(t2, 14))
                    down_tr(t, e, O(t2, 15))

            @blk.scalar
            def _(s):
                e = "scalar"
                PB.waitv(e, s, PBASE)
                for t2 in range(2):
                    for c in range(16):
                        i = t2 * 16 + c
                        sSA1.waitv(e, s, i + 1)
                        ins = s.copy(V1[:, c * 512:(c + 1) * 512], pSA[:, (c % 2) * 512:(c % 2 + 1) * 512])
                        if c < 15:
                            sSA1E.inc(e, ins, 1)
                        else:
                            sSA1E.inc(e, s.copy(DUMM[:, :], V_b[:, 0:64]), 1)
                    for j in range(64):
                        i = J(t2, j)
                        sUPM.waitv(e, s, i + 1)
                        sHMU.waitv(e, s, i - 1)
                        ins = s.activation(SG[:, (j % 2) * 512:(j % 2 + 1) * 512],
                                           pGA[:, (j % 2) * 512:(j % 2 + 1) * 512], AF.Silu,
                                           scale=V_b[:, C_SGATE + j:C_SGATE + j + 1])
                        sSIL.inc(e, ins, 1)
                    for j in range(64):
                        i = J(t2, j)
                        sSA2.waitv(e, s, i + 1)
                        ins = s.copy(HBUF[:, j * 512:(j + 1) * 512], pSA[:, (j % 2) * 512:(j % 2 + 1) * 512])
                        if j < 63:
                            sSA2E.inc(e, ins, 1)
                        else:
                            sSA2E.inc(e, s.copy(DUMM[:, :], V_b[:, 0:64]), 1)
                    for o in range(16):
                        i = O(t2, o)
                        sDNM.waitv(e, s, i + 1)
                        if i >= 2:
                            sDTR.waitv(e, s, i - 1)
                        ins = s.activation(DF[:, (i % 2) * 512:(i % 2 + 1) * 512],
                                           pDN[:, (i % 2) * 512:(i % 2 + 1) * 512],
                                           AF.Identity, scale=V_b[:, C_SDP + o:C_SDP + o + 1])
                        sDNE.inc(e, ins, 1)

            assert sHMU.n == 128 and sDNM.n == 32 and sACC.n == 32 and sOUT.n == 16 * 8

        rec.check()
        ctx.close()
        return nc
    except Exception:
        raise


def _make_masks(qblocks):
    m = np.zeros((40, 128, 128), np.float32)
    k = 0
    tri = np.tril(np.ones((128, 128), np.float32)).T
    for kb in range(16):
        for s in range(MSK_LO[kb], N_KB[kb]):
            qb, L = qblocks[s], SLOT_L[s]
            if kb < qb:
                m[k] = 1.0
            elif kb == qb:
                m[k] = tri
            k += 1
    assert k == 40
    return m.astype(bfdt)


_NC_CACHE = {}


def kernel(L0_attn_x, L0_attn_w_qkv, L0_attn_b_qkv, L0_attn_w_out, L0_attn_b_out,
           L0_ffn_SU_ug, L0_ffn_SU_dp, L0_ffn_Wscale_ug, L0_ffn_Wscale_dp,
           L0_ffn_W_up, L0_ffn_W_gate, L0_ffn_W_down):
    x = np.asarray(L0_attn_x, np.float32)
    shared = {
        "wqkvT": np.ascontiguousarray(np.asarray(L0_attn_w_qkv, np.float32).T),
        "woutT": np.ascontiguousarray(np.asarray(L0_attn_w_out, np.float32).T),
        "wupT": np.ascontiguousarray(np.asarray(L0_ffn_W_up, np.float32).T),
        "wgateT": np.ascontiguousarray(np.asarray(L0_ffn_W_gate, np.float32).T),
        "wdownT": np.ascontiguousarray(np.asarray(L0_ffn_W_down, np.float32).T),
        "bqkv_pc": np.ascontiguousarray(np.asarray(L0_attn_b_qkv, np.float32).reshape(48, 128).T),
        "bout_pc": np.ascontiguousarray(np.asarray(L0_attn_b_out, np.float32).reshape(16, 128).T),
        "suug_pc": np.ascontiguousarray(np.asarray(L0_ffn_SU_ug, np.float32).reshape(16, 128).T),
        "sudp_pc": np.ascontiguousarray(np.asarray(L0_ffn_SU_dp, np.float32).reshape(64, 128).T),
        "wsup_pc": np.ascontiguousarray(np.asarray(L0_ffn_Wscale_ug, np.float32)[:8192].reshape(64, 128).T),
        "wsgate_pc": np.ascontiguousarray(np.asarray(L0_ffn_Wscale_ug, np.float32)[8192:].reshape(64, 128).T),
        "wsdp_pc": np.ascontiguousarray(np.asarray(L0_ffn_Wscale_dp, np.float32).reshape(16, 128).T),
        "h128c": hadamard(128).astype(bfdt),
        "identf": np.eye(128, dtype=np.float32),
        "identb": np.eye(128).astype(bfdt),
        "onescol": np.ones((128, 1), bfdt),
        "negrow": np.full((1, 128), -1.0, np.float32),
        "epsc": np.full((128, 1), 1e-5, np.float32),
    }
    masks = {0: _make_masks(QB_EVEN).reshape(40 * 128, 128),
             1: _make_masks(QB_ODD).reshape(40 * 128, 128)}
    rows = {}
    in_maps = []
    for core in range(8):
        b, half = core // 2, core % 2
        qbs = QB_EVEN if half == 0 else QB_ODD
        r = np.concatenate([np.arange(qb * 128, (qb + 1) * 128) for qb in qbs])
        rows[core] = r
        m = dict(shared)
        m["x_full"] = np.ascontiguousarray(x[b])
        m["x_own"] = np.ascontiguousarray(x[b][r])
        m["maskst"] = masks[half]
        in_maps.append(m)

    import os
    dbg = int(os.environ.get("KERN_DEBUG", "0"))
    if "nc" not in _NC_CACHE:
        _NC_CACHE["nc"] = build_nc(debug=dbg)
    nc = _NC_CACHE["nc"]
    res = run_bass_kernel_spmd(nc, in_maps, core_ids=list(range(8)))
    outp = np.zeros((4, S, D), np.float32)
    for core in range(8):
        outp[core // 2][rows[core]] = np.asarray(res.results[core]["out"])
    if dbg:
        kernel.debug_results = res.results
        kernel.debug_rows = rows
    return outp


if __name__ == "__main__":
    nc = build_nc()
    print("built OK, instructions:", nc.n_instructions())


# version 02 (final)
# speedup vs baseline: 1.1260x; 1.1260x over previous
import math
import numpy as np
import ml_dtypes
import concourse.bass as bass
import concourse.mybir as mybir
from concourse.bass_utils import run_bass_kernel_spmd

F32 = mybir.dt.float32
BF16 = mybir.dt.bfloat16
AF = mybir.ActivationFunctionType
ALU = mybir.AluOpType
AX = mybir.AxisListType
bfdt = ml_dtypes.bfloat16

D = 2048
S = 2048
NH = 16
RS = 1.0 / math.sqrt(128.0)

QB_EVEN = [15, 14, 13, 12, 3, 2, 1, 0]
QB_ODD = [11, 10, 9, 8, 7, 6, 5, 4]
SLOT_L = [16, 15, 14, 13, 8, 7, 6, 5]
N_KB = [sum(1 for L in SLOT_L if L > kb) for kb in range(16)]
MSK_LO = [min(s for s, L in enumerate(SLOT_L) if L - 5 <= kb < L) for kb in range(16)]
MSK_N = [N_KB[kb] - MSK_LO[kb] for kb in range(16)]
MSK_OFF = np.cumsum([0] + MSK_N).tolist()
N_FOLD = 64
FOLD_COLS = 32


def hadamard(n):
    H = np.array([[1.0]], np.float32)
    while H.shape[0] < n:
        H = np.block([[H, H], [H, -H]])
    return np.ascontiguousarray(H.astype(np.float32))


class Rec:

    def __init__(self):
        self.streams = {e: [] for e in ("sync", "gpsimd", "tensor", "scalar", "vector")}

    def check(self):
        ptr = {e: 0 for e in self.streams}
        sems = {}
        progress = True
        while progress:
            progress = False
            for e, st in self.streams.items():
                while ptr[e] < len(st):
                    op, s, v = st[ptr[e]]
                    if op == "i":
                        sems[s] = sems.get(s, 0) + v
                        ptr[e] += 1
                        progress = True
                    elif sems.get(s, 0) >= v:
                        ptr[e] += 1
                        progress = True
                    else:
                        break
        stuck = {e: (ptr[e], len(st), self.streams[e][ptr[e]]) for e, st in self.streams.items()
                 if ptr[e] < len(st)}
        if stuck:
            raise RuntimeError(f"schedule deadlock: {stuck}")


class Sem:
    def __init__(self, nc, rec, name):
        self.h = nc.alloc_semaphore(name)
        self.name = name
        self.n = 0
        self.rec = rec

    def inc(self, ename, inst, k=1):
        inst.then_inc(self.h, k)
        self.n += k
        self.rec.streams[ename].append(("i", self.name, self.n))
        return inst

    def waitv(self, ename, eng, val):
        val = max(0, val)
        self.rec.streams[ename].append(("w", self.name, val))
        eng.wait_ge(self.h, val)

    def wait_all(self, ename, eng):
        self.waitv(ename, eng, self.n)


def _sem_inc_now(sem, ename, eng, k=1):
    eng.sem_inc(sem.h, k)
    sem.n += k
    sem.rec.streams[ename].append(("i", sem.name, sem.n))


def build_nc(debug=0):
    import contextlib
    nc = bass.Bass("TRN2", target_bir_lowering=False)
    rec = Rec()

    def dram_in(name, shape, dt=F32):
        return nc.dram_tensor(name, shape, dt, kind="ExternalInput").ap()

    x_full = dram_in("x_full", [S, D])
    x_own = dram_in("x_own", [1024, D])
    wqkvT = dram_in("wqkvT", [16, 128, 6144])
    woutT = dram_in("woutT", [16, 128, 2048])
    wupT = dram_in("wupT", [64, 128, 2048])
    wgateT = dram_in("wgateT", [64, 128, 2048])
    wdownT = dram_in("wdownT", [64, 128, 2048])
    bqkv_pc = dram_in("bqkv_pc", [128, 48])
    bout_pc = dram_in("bout_pc", [128, 16])
    suug_pc = dram_in("suug_pc", [128, 16])
    sudp_pc = dram_in("sudp_pc", [128, 64])
    wsup_pc = dram_in("wsup_pc", [128, 64])
    wsgate_pc = dram_in("wsgate_pc", [128, 64])
    wsdp_pc = dram_in("wsdp_pc", [128, 16])
    h128_in = dram_in("h128c", [128, 128], BF16)
    identf_in = dram_in("identf", [128, 128])
    identb_in = dram_in("identb", [128, 128], BF16)
    onescol_in = dram_in("onescol", [128, 1], BF16)
    negrow_in = dram_in("negrow", [1, 128])
    epsc_in = dram_in("epsc", [128, 1])
    masks_in = dram_in("maskst", [40 * 128, 128], BF16)

    out = nc.dram_tensor("out", [1024, D], F32, kind="ExternalOutput").ap()

    kd = "ExternalOutput" if debug else "Internal"
    scr_xnf = nc.dram_tensor("scr_xnf", [S, D], BF16, kind=kd).ap()
    scr_xno = nc.dram_tensor("scr_xno", [1024, D], BF16, kind=kd).ap()
    scr_attn = nc.dram_tensor("scr_attn", [1024, D], F32, kind=kd).ap()
    scr_md = nc.dram_tensor("scr_md", [N_FOLD, 128, 64 * FOLD_COLS], BF16, kind=kd).ap()
    dbg_hout = nc.dram_tensor("dbg_hout", [16 * 128, 1024], BF16, kind=kd).ap() if debug else None
    dbg_h = nc.dram_tensor("dbg_h", [64 * 128, 512], BF16, kind=kd).ap() if debug else None
    dbg_v1 = nc.dram_tensor("dbg_v1", [16 * 128, 512], BF16, kind=kd).ap() if debug else None
    dbg_v1b = nc.dram_tensor("dbg_v1b", [16 * 128, 512], BF16, kind=kd).ap() if debug else None
    dbg_hb = nc.dram_tensor("dbg_hb", [64 * 128, 512], BF16, kind=kd).ap() if debug else None

    sems = {}

    def mk(name):
        sems[name] = Sem(nc, rec, name)
        return sems[name]

    PB = mk("PB")
    mk("sDBG")
    RAW = mk("RAW")

    def raw_edge(ename, eng, inst):
        RAW.inc(ename, inst, 1)
        RAW.waitv(ename, eng, RAW.n)

    def phase_gate(blk, dma_waits=None):
        target = PB.n + 5

        def body(ename):
            def f(eng):
                for sem, val in (dma_waits or {}).get(ename, []):
                    sem.waitv(ename, eng, val)
                if ename in ("tensor", "scalar", "vector", "gpsimd"):
                    eng.drain()
                _sem_inc_now(PB, ename, eng)
                PB.waitv(ename, eng, target)
            return f
        blk.sync(body("sync"))
        blk.gpsimd(body("gpsimd"))
        blk.tensor(body("tensor"))
        blk.scalar(body("scalar"))
        blk.vector(body("vector"))
        assert PB.n == target

    ctx = contextlib.ExitStack()
    try:
        V_b = ctx.enter_context(nc.sbuf_tensor("V_b", [128, 305], F32, side="left"))
        C_BQ, C_BQS, C_BOUT, C_SUUG, C_SUDP, C_SUP, C_SGATE, C_SDP, C_EPS = 0, 48, 64, 80, 96, 160, 224, 288, 304
        H128s = ctx.enter_context(nc.sbuf_tensor("H128s", [128, 128], BF16, side="left"))
        IDB = ctx.enter_context(nc.sbuf_tensor("IDB", [128, 128], BF16, side="left"))
        IDF = ctx.enter_context(nc.sbuf_tensor("IDF", [128, 128], F32, side="left"))
        ONEC = ctx.enter_context(nc.sbuf_tensor("ONEC", [128, 1], BF16, side="left"))
        NEGR = ctx.enter_context(nc.sbuf_tensor("NEGR", [1, 128], F32, side="left"))
        DUMM = ctx.enter_context(nc.sbuf_tensor("DUMM", [128, 64], F32, side="left"))

        hout_ctx = contextlib.ExitStack()
        HOUT = hout_ctx.enter_context(nc.sbuf_tensor("HOUT", [128, 16 * 1024], BF16, side="left"))

        xnf_ctx = contextlib.ExitStack()
        xno = xnf_ctx.enter_context(nc.sbuf_tensor("xno", [128, 16 * 1024], BF16, side="right"))
        xnf = xnf_ctx.enter_context(nc.sbuf_tensor("xnf", [128, 16 * 2048], BF16, side="right"))

        sVLD = mk("sVLD")
        sBNS = mk("sBNS")
        sXT = mk("sXT")
        sSTAT = mk("sSTAT")
        sRSTD = mk("sRSTD")
        sXN = mk("sXN")
        sXNW = mk("sXNW")
        sXNT = mk("sXNT")
        NT = 24

        with (
            nc.sbuf_tensor("xstg", [128, 2 * 2048], F32, side="right") as xstg,
            nc.sbuf_tensor("xntg", [128, 2 * 2048], BF16, side="right") as xntg,
            nc.sbuf_tensor("bns", [128, 2 * 24], F32, side="right") as bns,
            nc.sbuf_tensor("mv", [128, 2 * 2], F32, side="right") as mv,
            nc.sbuf_tensor("rstd", [128, 2], F32, side="right") as rstd,
            nc.Block() as blk,
        ):
            @blk.sync
            def _(sy):
                e = "sync"
                for src, dst in ((h128_in, H128s), (identb_in, IDB), (identf_in, IDF),
                                 (onescol_in, ONEC), (negrow_in, NEGR)):
                    sVLD.inc(e, sy.dma_start(dst[:, :], src), 16)
                sVLD.inc(e, sy.dma_start(V_b[:, C_BQ:C_BQ + 48], bqkv_pc), 16)
                sVLD.inc(e, sy.dma_start(V_b[:, C_BOUT:C_BOUT + 16], bout_pc), 16)
                sVLD.inc(e, sy.dma_start(V_b[:, C_SUUG:C_SUUG + 16], suug_pc), 16)
                sVLD.inc(e, sy.dma_start(V_b[:, C_SUDP:C_SUDP + 64], sudp_pc), 16)
                sVLD.inc(e, sy.dma_start(V_b[:, C_SUP:C_SUP + 64], wsup_pc), 16)
                sVLD.inc(e, sy.dma_start(V_b[:, C_SGATE:C_SGATE + 64], wsgate_pc), 16)
                sVLD.inc(e, sy.dma_start(V_b[:, C_SDP:C_SDP + 16], wsdp_pc), 16)
                sVLD.inc(e, sy.dma_start(V_b[:, C_EPS:C_EPS + 1], epsc_in), 16)
                def wr(i):
                    sXN.waitv(e, sy, i + 1)
                    dstr = (scr_xnf[i * 128:(i + 1) * 128, :] if i < 16
                            else scr_xno[(i - 16) * 128:(i - 15) * 128, :])
                    sXNW.inc(e, sy.dma_start(dstr, xntg[:, (i % 2) * 2048:(i % 2 + 1) * 2048]), 16)

                for i in range(NT):
                    src = (x_full[i * 128:(i + 1) * 128, :] if i < 16
                           else x_own[(i - 16) * 128:(i - 15) * 128, :])
                    if i >= 2:
                        sXN.waitv(e, sy, i - 1)
                    sXT.inc(e, sy.dma_start(xstg[:, (i % 2) * 2048:(i % 2 + 1) * 2048], src), 16)
                    if i >= 1:
                        wr(i - 1)
                wr(NT - 1)
                sXNW.waitv(e, sy, 16 * 16)
                for c in range(16):
                    sXNT.inc(e, sy.dma_start_transpose(xnf[:, c * 2048:(c + 1) * 2048],
                                                       scr_xnf[:, c * 128:(c + 1) * 128]), 16)
                sXNW.waitv(e, sy, 16 * 24)
                for c in range(16):
                    sXNT.inc(e, sy.dma_start_transpose(xno[:, c * 1024:(c + 1) * 1024],
                                                       scr_xno[:, c * 128:(c + 1) * 128]), 16)

            @blk.vector
            def _(v):
                e = "vector"
                sVLD.waitv(e, v, 16 * 13)
                v.tensor_scalar(V_b[:, C_BQS:C_BQS + 16], V_b[:, C_BQ:C_BQ + 16], RS, None, ALU.mult)
                sBNS.inc(e, v.tensor_scalar(V_b[:, C_SUP:C_SUP + 64], V_b[:, C_SUP:C_SUP + 64],
                                            1.0 / math.sqrt(2048.0), None, ALU.mult), 1)
                sBNS.waitv(e, v, sBNS.n)
                v.tensor_tensor(V_b[:, C_SUP:C_SUP + 64], V_b[:, C_SUP:C_SUP + 64],
                                V_b[:, C_SUDP:C_SUDP + 64], ALU.mult)
                v.tensor_scalar(V_b[:, C_SGATE:C_SGATE + 64], V_b[:, C_SGATE:C_SGATE + 64],
                                1.0 / math.sqrt(2048.0), None, ALU.mult)
                v.tensor_scalar(V_b[:, C_SDP:C_SDP + 16], V_b[:, C_SDP:C_SDP + 16],
                                1.0 / math.sqrt(8192.0), None, ALU.mult)
                for i in range(NT):
                    p = i % 2
                    sXT.waitv(e, v, 16 * (i + 1))
                    if i >= 2:
                        sXNW.waitv(e, v, 16 * (i - 1))
                        sRSTD.waitv(e, v, i - 1)
                    for q in range(4):
                        sBNS.inc(e, v.bn_stats(bns[:, p * 24 + q * 6:p * 24 + (q + 1) * 6],
                                               xstg[:, p * 2048 + q * 512:p * 2048 + (q + 1) * 512]), 1)
                    sBNS.waitv(e, v, 1 + 4 * (i + 1))
                    sSTAT.inc(e, v.bn_aggr(mv[:, p * 2:(p + 1) * 2], bns[:, p * 24:(p + 1) * 24]), 1)
                    sRSTD.waitv(e, v, i + 1)
                    sXN.inc(e, v.tensor_scalar(xntg[:, p * 2048:(p + 1) * 2048],
                                               xstg[:, p * 2048:(p + 1) * 2048],
                                               mv[:, p * 2:p * 2 + 1], rstd[:, p:p + 1],
                                               ALU.subtract, ALU.mult), 1)

            @blk.scalar
            def _(s):
                e = "scalar"
                for i in range(NT):
                    p = i % 2
                    sSTAT.waitv(e, s, i + 1)
                    if i >= 2:
                        sXN.waitv(e, s, i - 1)
                    sVLD.waitv(e, s, 16 * 13)
                    raw_edge(e, s, s.activation(rstd[:, p:p + 1], mv[:, p * 2 + 1:p * 2 + 2], AF.Ln,
                                                bias=V_b[:, C_EPS:C_EPS + 1]))
                    sRSTD.inc(e, s.activation(rstd[:, p:p + 1], rstd[:, p:p + 1], AF.Exp, scale=-0.5), 1)

        with nc.Block() as blk:
            phase_gate(blk, dma_waits={"sync": [(sXNT, 16 * 32)]})

        sWQ = mk("sWQ")
        sMSK = mk("sMSK")
        sPEQ = mk("sPEQ")
        sSEV = mk("sSEV")
        sTMM = mk("sTMM")
        sTEV = mk("sTEV")
        sSTM = mk("sSTM")
        sEXP = mk("sEXP")
        sPMK = mk("sPMK")
        sAVD = mk("sAVD")
        sRCP = mk("sRCP")
        sBCT = mk("sBCT")
        sBEX = mk("sBEX")
        sHOE = mk("sHOE")
        sFIN = mk("sFIN")
        sFLD = mk("sFLD")
        sFOD = mk("sFOD")

        with (
            nc.sbuf_tensor("wq", [128, 3 * 16 * 128], BF16, side="right") as wq,
            nc.sbuf_tensor("KT", [128, 2 * 2048], BF16, side="right") as KT,
            nc.sbuf_tensor("VTf", [128, 2048], BF16, side="right") as VTf,
            nc.sbuf_tensor("Vtm", [128, 2 * 2048], BF16, side="right") as Vtm,
            nc.sbuf_tensor("QT", [128, 2 * 1024], BF16, side="right") as QT,
            nc.sbuf_tensor("MASKS", [128, 40 * 128], BF16, side="right") as MASKS,
            nc.sbuf_tensor("PT", [128, 2 * 1024], BF16, side="right") as PT,
            nc.sbuf_tensor("BREC", [128, 1024], F32, side="right") as BREC,
            nc.sbuf_tensor("LNL", [1, 1024], F32, side="right") as LNL,
            nc.sbuf_tensor("FA", [128, 64 * FOLD_COLS], F32, side="right") as FA,
            nc.sbuf_tensor("FB", [128, 64 * FOLD_COLS], F32, side="right") as FB,
            nc.psum_tensor("pST", [128, 1024], F32) as pST,
            nc.psum_tensor("pHO", [128, 1024], F32) as pHO,
            nc.psum_tensor("pL", [1, 1024], F32) as pL,
            nc.psum_tensor("pQKV", [128, 512], F32) as pQKV,
            nc.psum_tensor("pVT", [128, 1024], BF16) as pVT,
            nc.Block() as blk,
        ):
            def fold_piece(eng, ename):
                bufs = [FA, FB]
                ins = None
                for lvl in range(6):
                    src, dst = bufs[lvl % 2], bufs[(lvl + 1) % 2]
                    half = 1 << lvl
                    sv = src[:, :].rearrange("p (a two b c) -> p a two b c",
                                             two=2, b=half, c=FOLD_COLS)
                    dv = dst[:, :].rearrange("p (a two b c) -> p a two b c",
                                             two=2, b=half, c=FOLD_COLS)
                    eng.tensor_tensor(dv[:, :, 0, :, :], sv[:, :, 0, :, :], sv[:, :, 1, :, :], ALU.add)
                    ins = eng.tensor_tensor(dv[:, :, 1, :, :], sv[:, :, 0, :, :], sv[:, :, 1, :, :], ALU.subtract)
                    if lvl < 5:
                        raw_edge(ename, eng, ins)
                sFLD.inc(ename, ins, 1)

            @blk.gpsimd
            def _(g):
                e = "gpsimd"
                PB.waitv(e, g, PB.n)
                fold_out = [0]

                def emit_fold_outs(upto):
                    while fold_out[0] < min(upto, N_FOLD):
                        i = fold_out[0]
                        sFLD.waitv(e, g, i + 1)
                        sFOD.inc(e, g.dma_start(scr_md[i, :, :], FA[:, :]), 16)
                        fold_out[0] += 1

                for h in range(16):
                    if h >= 1:
                        sPEQ.waitv(e, g, 10 * h)
                    ins = g.dma_start(wq[:, :], wqkvT[h, :, :])
                    sWQ.inc(e, ins, 48)
                    emit_fold_outs(4 * (h - 1))
                emit_fold_outs(N_FOLD)

            @blk.sync
            def _(sy):
                e = "sync"
                PB.waitv(e, sy, PB.n)
                sMSK.inc(e, sy.dma_start(MASKS[:, :].rearrange("p (t m) -> p t m", m=128),
                                         masks_in.rearrange("(t p) m -> p t m", p=128)), 16)
                for i in range(N_FOLD):
                    if i >= 1:
                        sFOD.waitv(e, sy, 16 * i)
                    sFIN.inc(e, sy.dma_start(FA[:, :], wdownT[i, :, :]), 16)

            @blk.tensor
            def _(t):
                e = "tensor"
                PB.waitv(e, t, PB.n)

                def qkv_group(h, g):
                    sWQ.waitv(e, t, 16 * 3 * (h + 1))
                    sSEV.waitv(e, t, 10 * h + g)
                    wcol = (0 if g < 2 else (16 * 128 if g < 6 else 32 * 128))
                    tg = g if g < 2 else (g - 2 if g < 6 else g - 6)
                    ins = None
                    for kc in range(16):
                        lhs = wq[:, wcol + kc * 128:wcol + (kc + 1) * 128]
                        if g >= 2:
                            rhs = xnf[:, kc * 2048 + tg * 512: kc * 2048 + (tg + 1) * 512]
                        else:
                            rhs = xno[:, kc * 1024 + tg * 512: kc * 1024 + (tg + 1) * 512]
                        ins = t.matmul(pQKV[:, :], lhs, rhs, start=(kc == 0), stop=(kc == 15))
                    sPEQ.inc(e, ins, 1)

                def v_transposes(h, b):
                    sSEV.waitv(e, t, 10 * h + 8 + 2 * b)
                    sTEV.waitv(e, t, 2 * h + b)
                    ins = None
                    for tt in range(8):
                        ins = t.transpose(pVT[:, tt * 128:(tt + 1) * 128],
                                          VTf[:, (b * 8 + tt) * 128:(b * 8 + tt + 1) * 128], IDB[:, :])
                    sTMM.inc(e, ins, 1)

                def do_avl(h, kb):
                    par = h % 2
                    n = N_KB[kb]
                    w = n * 128
                    sPMK.waitv(e, t, 16 * h + kb + 1)
                    if kb == 0:
                        sHOE.waitv(e, t, 2 * h)
                        sRCP.waitv(e, t, h)
                    pcol = (kb % 2) * 1024
                    t.matmul(pHO[:, 0:min(w, 512)],
                             Vtm[:, par * 2048 + kb * 128: par * 2048 + (kb + 1) * 128],
                             PT[:, pcol:pcol + min(w, 512)], start=(kb == 0), stop=(kb == 15))
                    ins = t.matmul(pL[:, 0:min(w, 512)], ONEC[:, :], PT[:, pcol:pcol + min(w, 512)],
                                    start=(kb == 0), stop=(kb == 15))
                    if w > 512:
                        t.matmul(pHO[:, 512:w],
                                 Vtm[:, par * 2048 + kb * 128: par * 2048 + (kb + 1) * 128],
                                 PT[:, pcol + 512:pcol + w], start=(kb == 0), stop=(kb == 7))
                        ins = t.matmul(pL[:, 512:w], ONEC[:, :], PT[:, pcol + 512:pcol + w],
                                       start=(kb == 0), stop=(kb == 7))
                    sAVD.inc(e, ins, 1)

                for g in range(10):
                    qkv_group(0, g)
                for b in range(2):
                    v_transposes(0, b)

                for h in range(16):
                    par = h % 2
                    for kb in range(16):
                        n = N_KB[kb]
                        w = n * 128
                        if h + 1 < 16:
                            if kb < 10:
                                qkv_group(h + 1, kb)
                            elif kb in (10, 11):
                                v_transposes(h + 1, kb - 10)
                        if kb == 0:
                            sSEV.waitv(e, t, 10 * h + 10)
                            sTEV.waitv(e, t, 2 * h + 2)
                            sBEX.waitv(e, t, h)
                        else:
                            sEXP.waitv(e, t, 16 * h + kb)
                        lhsk = KT[:, par * 2048 + kb * 128: par * 2048 + (kb + 1) * 128]
                        ins = t.matmul(pST[:, 0:min(w, 512)], lhsk,
                                       QT[:, par * 1024: par * 1024 + min(w, 512)], start=True, stop=True)
                        if w > 512:
                            ins = t.matmul(pST[:, 512:w], lhsk,
                                           QT[:, par * 1024 + 512: par * 1024 + w], start=True, stop=True)
                        sSTM.inc(e, ins, 1)
                        if kb >= 1:
                            do_avl(h, kb - 1)
                    do_avl(h, 15)
                    sRCP.waitv(e, t, h + 1)
                    sEXP.waitv(e, t, 16 * h + 16)
                    t.matmul(pST[:, 0:512], NEGR[:, :], LNL[:, 0:512], start=True, stop=True)
                    ins = t.matmul(pST[:, 512:1024], NEGR[:, :], LNL[:, 512:1024], start=True, stop=True)
                    sBCT.inc(e, ins, 1)

            @blk.scalar
            def _(s):
                e = "scalar"
                PB.waitv(e, s, PB.n)

                def qkv_evac(h, g):
                    par = h % 2
                    tg = g if g < 2 else (g - 2 if g < 6 else g - 6)
                    sPEQ.waitv(e, s, 10 * h + g + 1)
                    if g == 6:
                        sTMM.waitv(e, s, 2 * h)
                    if g < 2:
                        ins = s.activation(QT[:, par * 1024 + tg * 512: par * 1024 + (tg + 1) * 512],
                                           pQKV[:, :], AF.Identity, bias=V_b[:, C_BQS + h:C_BQS + h + 1], scale=RS)
                    elif g < 6:
                        ins = s.activation(KT[:, par * 2048 + tg * 512: par * 2048 + (tg + 1) * 512],
                                           pQKV[:, :], AF.Identity, bias=V_b[:, C_BQ + 16 + h:C_BQ + 17 + h])
                    else:
                        ins = s.activation(VTf[:, tg * 512:(tg + 1) * 512],
                                           pQKV[:, :], AF.Identity, bias=V_b[:, C_BQ + 32 + h:C_BQ + 33 + h])
                    sSEV.inc(e, ins, 1)

                def v_tevac(h, b):
                    par = h % 2
                    sTMM.waitv(e, s, 2 * h + b + 1)
                    ins = None
                    for tt in range(8):
                        ins = s.copy(Vtm[:, par * 2048 + (b * 8 + tt) * 128: par * 2048 + (b * 8 + tt + 1) * 128],
                                     pVT[:, tt * 128:(tt + 1) * 128])
                    sTEV.inc(e, ins, 1)

                for g in range(10):
                    qkv_evac(0, g)
                for b in range(2):
                    v_tevac(0, b)

                for h in range(16):
                    for kb in range(16):
                        if h + 1 < 16:
                            if kb < 10:
                                qkv_evac(h + 1, kb)
                            elif kb in (10, 11):
                                v_tevac(h + 1, kb - 10)
                        n = N_KB[kb]
                        sSTM.waitv(e, s, 16 * h + kb + 1)
                        sAVD.waitv(e, s, 16 * h + kb - 1)
                        ins = s.activation(PT[:, (kb % 2) * 1024:(kb % 2) * 1024 + n * 128],
                                           pST[:, 0:n * 128], AF.Exp)
                        sEXP.inc(e, ins, 1)
                    sAVD.waitv(e, s, 16 * h + 16)
                    sBCT.waitv(e, s, h)
                    sRCP.inc(e, s.activation(LNL[:, :], pL[:, :], AF.Ln), 1)
                    sBCT.waitv(e, s, h + 1)
                    sHOE.waitv(e, s, 2 * h)
                    sBEX.inc(e, s.activation(BREC[:, :], pST[:, 0:1024], AF.Exp), 1)

            @blk.vector
            def _(v):
                e = "vector"
                PB.waitv(e, v, PB.n)
                sMSK.waitv(e, v, 16)
                fold_next = [0]

                def fold_some(upto):
                    while fold_next[0] < min(upto, N_FOLD):
                        i = fold_next[0]
                        sFIN.waitv(e, v, 16 * (i + 1))
                        fold_piece(v, e)
                        fold_next[0] += 1

                for h in range(16):
                    for kb in range(16):
                        n, lo = N_KB[kb], MSK_LO[kb]
                        pcol = (kb % 2) * 1024
                        sEXP.waitv(e, v, 16 * h + kb + 1)
                        ins = v.tensor_tensor(PT[:, pcol + lo * 128:pcol + n * 128],
                                              PT[:, pcol + lo * 128:pcol + n * 128],
                                              MASKS[:, MSK_OFF[kb] * 128:MSK_OFF[kb + 1] * 128], ALU.mult)
                        sPMK.inc(e, ins, 1)
                    sBEX.waitv(e, v, h + 1)
                    sAVD.waitv(e, v, 16 * h + 16)
                    for half in range(2):
                        ins = v.tensor_tensor(HOUT[:, h * 1024 + half * 512: h * 1024 + (half + 1) * 512],
                                              pHO[:, half * 512:(half + 1) * 512],
                                              BREC[:, half * 512:(half + 1) * 512], ALU.mult)
                        sHOE.inc(e, ins, 1)
                    fold_some(4 * h)
                fold_some(N_FOLD)

            assert sPEQ.n == 160 and sSEV.n == 160 and sEXP.n == 256 and sAVD.n == 256
            assert sFLD.n == N_FOLD and sHOE.n == 32

        with nc.Block() as blk:
            phase_gate(blk, dma_waits={"gpsimd": [(sFOD, 16 * N_FOLD)], "sync": [(sFIN, 16 * N_FOLD)]})
        xnf_ctx.close()

        sWO = mk("sWO")
        sOPM = mk("sOPM")
        sOPE = mk("sOPE")
        sOTR = mk("sOTR")
        sOTC = mk("sOTC")
        sATW = mk("sATW")

        with (
            nc.sbuf_tensor("wo", [128, 2 * 16 * 128], BF16, side="right") as wo,
            nc.sbuf_tensor("SF", [128, 2 * 512], F32, side="right") as SF,
            nc.sbuf_tensor("ATM", [128, 4 * 2048], F32, side="right") as ATM,
            nc.psum_tensor("pOP", [128, 2 * 512], F32) as pOP,
            nc.psum_tensor("pTR", [128, 2 * 512], F32) as pTR,
            nc.Block() as blk,
        ):
            def idx(tg, oc):
                return tg * 16 + oc

            @blk.gpsimd
            def _(g):
                e = "gpsimd"
                PB.waitv(e, g, PB.n)
                for tg in range(2):
                    for oc in range(16):
                        i = idx(tg, oc)
                        sOPM.waitv(e, g, i - 1)
                        ins = g.dma_start(wo[:, (i % 2) * 2048:(i % 2 + 1) * 2048], woutT[oc, :, :])
                        sWO.inc(e, ins, 16)

            def do_tr(t, e, i):
                sOPE.waitv(e, t, i + 1)
                sOTC.waitv(e, t, 4 * (i - 1))
                ins = None
                for tt in range(4):
                    ins = t.transpose(pTR[:, (i % 2) * 512 + tt * 128:(i % 2) * 512 + (tt + 1) * 128],
                                      SF[:, (i % 2) * 512 + tt * 128:(i % 2) * 512 + (tt + 1) * 128], IDF[:, :])
                sOTR.inc(e, ins, 1)

            @blk.tensor
            def _(t):
                e = "tensor"
                PB.waitv(e, t, PB.n)
                for tg in range(2):
                    for oc in range(16):
                        i = idx(tg, oc)
                        sWO.waitv(e, t, 16 * (i + 1))
                        sOPE.waitv(e, t, i - 1)
                        ins = None
                        for hc in range(16):
                            ins = t.matmul(pOP[:, (i % 2) * 512:(i % 2 + 1) * 512],
                                           wo[:, (i % 2) * 2048 + hc * 128:(i % 2) * 2048 + (hc + 1) * 128],
                                           HOUT[:, hc * 1024 + tg * 512: hc * 1024 + (tg + 1) * 512],
                                           start=(hc == 0), stop=(hc == 15))
                        sOPM.inc(e, ins, 1)
                        if i >= 1:
                            do_tr(t, e, i - 1)
                do_tr(t, e, 31)

            @blk.scalar
            def _(s):
                e = "scalar"
                PB.waitv(e, s, PB.n)
                for tg in range(2):
                    for oc in range(16):
                        i = idx(tg, oc)
                        sOPM.waitv(e, s, i + 1)
                        if i >= 2:
                            sOTR.waitv(e, s, i - 1)
                        ins = s.activation(SF[:, (i % 2) * 512:(i % 2 + 1) * 512],
                                           pOP[:, (i % 2) * 512:(i % 2 + 1) * 512],
                                           AF.Identity, bias=V_b[:, C_BOUT + oc:C_BOUT + oc + 1])
                        sOPE.inc(e, ins, 1)

            @blk.vector
            def _(v):
                e = "vector"
                PB.waitv(e, v, PB.n)
                for tg in range(2):
                    for oc in range(16):
                        i = idx(tg, oc)
                        oc_ = i % 16
                        sOTR.waitv(e, v, i + 1)
                        if tg == 1 and oc == 0:
                            sATW.waitv(e, v, 16 * 4)
                        ins = None
                        for tt in range(4):
                            ins = v.tensor_copy(ATM[:, tt * 2048 + oc_ * 128: tt * 2048 + (oc_ + 1) * 128],
                                                pTR[:, (i % 2) * 512 + tt * 128:(i % 2) * 512 + (tt + 1) * 128])
                        sOTC.inc(e, ins, 4)

            @blk.sync
            def _(sy):
                e = "sync"
                PB.waitv(e, sy, PB.n)
                if debug:
                    for hh_ in range(16):
                        sems["sDBG"].inc(e, sy.dma_start(dbg_hout[hh_ * 128:(hh_ + 1) * 128, :],
                                                         HOUT[:, hh_ * 1024:(hh_ + 1) * 1024]), 16)
                for tg in range(2):
                    sOTC.waitv(e, sy, (tg + 1) * 64)
                    for tt in range(4):
                        sATW.inc(e, sy.dma_start(scr_attn[tg * 512 + tt * 128: tg * 512 + (tt + 1) * 128, :],
                                                 ATM[:, tt * 2048:(tt + 1) * 2048]), 16)

        with nc.Block() as blk:
            phase_gate(blk, dma_waits={"sync": [(sATW, 16 * 8)], "gpsimd": [(sWO, 16 * 32)]})
        hout_ctx.close()

        sXNH = mk("sXNH")
        sACX = mk("sACX")
        sACA = mk("sACA")
        sV1B = mk("sV1B")
        sSA1 = mk("sSA1")
        sSA1E = mk("sSA1E")
        sUGD = mk("sUGD")
        sUPM = mk("sUPM")
        sSIL = mk("sSIL")
        sHMU = mk("sHMU")
        sSA2 = mk("sSA2")
        sSA2E = mk("sSA2E")
        sMDL = mk("sMDL")
        sMDR = mk("sMDR")
        sDNM = mk("sDNM")
        sDNE = mk("sDNE")
        sDTR = mk("sDTR")
        sACC = mk("sACC")
        sOUT = mk("sOUT")

        with (
            nc.sbuf_tensor("XNOH", [128, 16 * 512], BF16, side="right") as XNOH,
            nc.sbuf_tensor("ACC", [128, 4 * 2048], F32, side="right") as ACC,
            nc.sbuf_tensor("HBUF", [128, 64 * 512], BF16, side="right") as HBUF,
            nc.sbuf_tensor("V1", [128, 16 * 512], BF16, side="right") as V1,
            nc.sbuf_tensor("WppA", [128, 1024], F32, side="right") as WppA,
            nc.sbuf_tensor("WppB", [128, 1024], F32, side="right") as WppB,
            nc.sbuf_tensor("WU", [128, 2 * 2048], BF16, side="right") as WU,
            nc.sbuf_tensor("WG", [128, 2 * 2048], BF16, side="right") as WG,
            nc.sbuf_tensor("SG", [128, 2 * 512], F32, side="right") as SG,
            nc.sbuf_tensor("MDRAW", [128, 4096], BF16, side="right") as MDRAW,
            nc.sbuf_tensor("MD", [128, 2 * 8192], BF16, side="right") as MD,
            nc.sbuf_tensor("DF", [128, 2 * 512], F32, side="right") as DF,
            nc.psum_tensor("pSA", [128, 2 * 512], F32) as pSA,
            nc.psum_tensor("pUP", [128, 2 * 512], F32) as pUP,
            nc.psum_tensor("pGA", [128, 2 * 512], F32) as pGA,
            nc.psum_tensor("pDN", [128, 2 * 512], F32) as pDN,
            nc.Block() as blk,
        ):
            PBASE = PB.n

            def J(t2, j):
                return t2 * 64 + j

            def O(t2, o):
                return t2 * 16 + o

            @blk.sync
            def _(sy):
                e = "sync"
                PB.waitv(e, sy, PBASE)
                for t2 in range(2):
                    if debug and t2 == 1:
                        pass
                    if t2 == 1:
                        sV1B.waitv(e, sy, 8)
                    for c in range(16):
                        sXNH.inc(e, sy.dma_start_transpose(
                            XNOH[:, c * 512:(c + 1) * 512],
                            scr_xno[t2 * 512:(t2 + 1) * 512, c * 128:(c + 1) * 128]), 16)
                    if debug and t2 == 0:
                        sSA1E.waitv(e, sy, 16)
                        for cc_ in range(16):
                            sems["sDBG"].inc(e, sy.dma_start(dbg_v1[cc_ * 128:(cc_ + 1) * 128, :],
                                                             V1[:, cc_ * 512:(cc_ + 1) * 512]), 16)
                    for tt in range(4):
                        if t2 == 1:
                            sOUT.waitv(e, sy, 16 * 4)
                        sACX.inc(e, sy.dma_start(ACC[:, tt * 2048:(tt + 1) * 2048],
                                                 x_own[t2 * 512 + tt * 128: t2 * 512 + (tt + 1) * 128, :]), 16)
                    for o in range(16):
                        for hh in range(2):
                            i2 = O(t2, o) * 2 + hh
                            if i2 >= 1:
                                sMDR.waitv(e, sy, i2)
                            for qq in range(2):
                                sMDL.inc(e, sy.dma_start(MDRAW[:, qq * 2048:(qq + 1) * 2048],
                                                         scr_md[4 * o + hh * 2 + qq, :, :]), 16)
                    if debug:
                        sSA2E.waitv(e, sy, 64 * (t2 + 1))
                        dh = dbg_h if t2 == 0 else dbg_hb
                        for jj_ in range(64):
                            sems["sDBG"].inc(e, sy.dma_start(dh[jj_ * 128:(jj_ + 1) * 128, :],
                                                             HBUF[:, jj_ * 512:(jj_ + 1) * 512]), 16)
                        if t2 == 1:
                            sSA1E.waitv(e, sy, 32)
                            for cc_ in range(16):
                                sems["sDBG"].inc(e, sy.dma_start(dbg_v1b[cc_ * 128:(cc_ + 1) * 128, :],
                                                                 V1[:, cc_ * 512:(cc_ + 1) * 512]), 16)
                    for tt in range(4):
                        sACC.waitv(e, sy, 16 * (t2 + 1))
                        sOUT.inc(e, sy.dma_start(out[t2 * 512 + tt * 128: t2 * 512 + (tt + 1) * 128, :],
                                                 ACC[:, tt * 2048:(tt + 1) * 2048]), 16)
                if debug:
                    sems["sDBG"].wait_all(e, sy)
                sOUT.waitv(e, sy, 16 * 8)

            @blk.gpsimd
            def _(g):
                e = "gpsimd"
                PB.waitv(e, g, PBASE)
                for t2 in range(2):
                    for tt in range(4):
                        sACX.waitv(e, g, 16 * (t2 * 4 + tt + 1))
                        sACA.inc(e, g.dma_start(ACC[:, tt * 2048:(tt + 1) * 2048],
                                                scr_attn[t2 * 512 + tt * 128: t2 * 512 + (tt + 1) * 128, :],
                                                accum_op=ALU.add), 16)
                    for j in range(64):
                        i = J(t2, j)
                        sHMU.waitv(e, g, i - 1)
                        sUGD.inc(e, g.dma_start(WU[:, (j % 2) * 2048:(j % 2 + 1) * 2048], wupT[j, :, :]), 16)
                        sUGD.inc(e, g.dma_start(WG[:, (j % 2) * 2048:(j % 2 + 1) * 2048], wgateT[j, :, :]), 16)
                    for o in range(16):
                        i = O(t2, o)
                        for hh in range(2):
                            i2 = i * 2 + hh
                            sMDL.waitv(e, g, 32 * (i2 + 1))
                            if hh == 0 and i >= 2:
                                sDNM.waitv(e, g, i - 1)
                            srcv = MDRAW[:, :].rearrange("p (qq c m) -> p c qq m", qq=2, m=FOLD_COLS)
                            dstv = MD[:, (i % 2) * 8192:(i % 2 + 1) * 8192].rearrange(
                                "p (c qq2 m) -> p c qq2 m", qq2=4, m=FOLD_COLS)[:, :, hh * 2:(hh + 1) * 2, :]
                            sMDR.inc(e, g.tensor_copy(dstv, srcv), 1)

            @blk.vector
            def _(v):
                e = "vector"
                PB.waitv(e, v, PBASE)
                for t2 in range(2):
                    for pc in range(8):
                        if pc == 0:
                            sXNH.waitv(e, v, 16 * 16 * (t2 + 1))
                        if t2 == 1 and pc == 0:
                            sUPM.waitv(e, v, 64)
                        ins = None
                        for c in range(16):
                            ins = v.tensor_scalar(WppA[:, c * 64:(c + 1) * 64],
                                                  XNOH[:, c * 512 + pc * 64: c * 512 + (pc + 1) * 64],
                                                  V_b[:, C_SUUG + c:C_SUUG + c + 1], None, ALU.mult)
                        raw_edge(e, v, ins)
                        bufs = [WppA, WppB]
                        for lvl in range(4):
                            srcb, dstb = bufs[lvl % 2], bufs[(lvl + 1) % 2]
                            half = 1 << lvl
                            sv = srcb[:, :].rearrange("p (a two b c) -> p a two b c", two=2, b=half, c=64)
                            if lvl < 3:
                                dv = dstb[:, :].rearrange("p (a two b c) -> p a two b c", two=2, b=half, c=64)
                                v.tensor_tensor(dv[:, :, 0, :, :], sv[:, :, 0, :, :], sv[:, :, 1, :, :], ALU.add)
                                ins = v.tensor_tensor(dv[:, :, 1, :, :], sv[:, :, 0, :, :], sv[:, :, 1, :, :], ALU.subtract)
                                raw_edge(e, v, ins)
                            else:
                                dv = V1[:, :].rearrange("p (a two b c) -> p a two b c", two=2, b=half, c=512)[
                                    :, :, :, :, pc * 64:(pc + 1) * 64]
                                v.tensor_tensor(dv[:, :, 0, :, :], sv[:, :, 0, :, :], sv[:, :, 1, :, :], ALU.add)
                                ins = v.tensor_tensor(dv[:, :, 1, :, :], sv[:, :, 0, :, :], sv[:, :, 1, :, :], ALU.subtract)
                        if pc < 7:
                            sV1B.inc(e, ins, 1)
                        else:
                            sV1B.inc(e, v.tensor_copy(WppB[:, 0:512], WppA[:, 0:512]), 1)
                    for j in range(64):
                        i = J(t2, j)
                        sSIL.waitv(e, v, i + 1)
                        ins = v.scalar_tensor_tensor(HBUF[:, j * 512:(j + 1) * 512],
                                                     pUP[:, (j % 2) * 512:(j % 2 + 1) * 512],
                                                     V_b[:, C_SUP + j:C_SUP + j + 1],
                                                     SG[:, (j % 2) * 512:(j % 2 + 1) * 512],
                                                     ALU.mult, ALU.mult)
                        sHMU.inc(e, ins, 1)
                    for o in range(16):
                        i = O(t2, o)
                        sDTR.waitv(e, v, i + 1)
                        if o == 0:
                            sACA.waitv(e, v, 16 * 4 * (t2 + 1))
                        ins = None
                        for tt in range(4):
                            ins = v.tensor_tensor(ACC[:, tt * 2048 + o * 128: tt * 2048 + (o + 1) * 128],
                                                  ACC[:, tt * 2048 + o * 128: tt * 2048 + (o + 1) * 128],
                                                  pSA[:, (i % 2) * 512 + tt * 128:(i % 2) * 512 + (tt + 1) * 128],
                                                  ALU.add)
                        sACC.inc(e, ins, 1)

            def down_tr(t, e, i):
                sDNE.waitv(e, t, i + 1)
                sACC.waitv(e, t, i - 1)
                ins = None
                for tt in range(4):
                    ins = t.transpose(pSA[:, (i % 2) * 512 + tt * 128:(i % 2) * 512 + (tt + 1) * 128],
                                      DF[:, (i % 2) * 512 + tt * 128:(i % 2) * 512 + (tt + 1) * 128], IDF[:, :])
                sDTR.inc(e, ins, 1)

            @blk.tensor
            def _(t):
                e = "tensor"
                PB.waitv(e, t, PBASE)
                for t2 in range(2):
                    for c in range(16):
                        i = t2 * 16 + c
                        sV1B.waitv(e, t, 8 * (t2 + 1))
                        if t2 == 1 and c == 0:
                            sACC.waitv(e, t, 16)
                        sSA1E.waitv(e, t, i - 1)
                        ins = t.matmul(pSA[:, (c % 2) * 512:(c % 2 + 1) * 512], H128s[:, :],
                                       V1[:, c * 512:(c + 1) * 512], start=True, stop=True)
                        sSA1.inc(e, ins, 1)
                    for j in range(64):
                        i = J(t2, j)
                        sUGD.waitv(e, t, 32 * (i + 1))
                        sSA1E.waitv(e, t, 16 * (t2 + 1))
                        sHMU.waitv(e, t, i - 1)
                        sSIL.waitv(e, t, i - 1)
                        for kc in range(16):
                            t.matmul(pUP[:, (j % 2) * 512:(j % 2 + 1) * 512],
                                     WU[:, (j % 2) * 2048 + kc * 128:(j % 2) * 2048 + (kc + 1) * 128],
                                     V1[:, kc * 512:(kc + 1) * 512], start=(kc == 0), stop=(kc == 15))
                        ins = None
                        for kc in range(16):
                            ins = t.matmul(pGA[:, (j % 2) * 512:(j % 2 + 1) * 512],
                                           WG[:, (j % 2) * 2048 + kc * 128:(j % 2) * 2048 + (kc + 1) * 128],
                                           V1[:, kc * 512:(kc + 1) * 512], start=(kc == 0), stop=(kc == 15))
                        sUPM.inc(e, ins, 1)
                    for j in range(64):
                        i = J(t2, j)
                        sHMU.waitv(e, t, i + 1)
                        sSA2E.waitv(e, t, i - 1)
                        ins = t.matmul(pSA[:, (j % 2) * 512:(j % 2 + 1) * 512], H128s[:, :],
                                       HBUF[:, j * 512:(j + 1) * 512], start=True, stop=True)
                        sSA2.inc(e, ins, 1)
                    for o in range(16):
                        i = O(t2, o)
                        sMDR.waitv(e, t, 2 * (i + 1))
                        sSA2E.waitv(e, t, 64 * (t2 + 1))
                        sDNE.waitv(e, t, i - 1)
                        ins = None
                        for c in range(64):
                            ins = t.matmul(pDN[:, (i % 2) * 512:(i % 2 + 1) * 512],
                                           MD[:, (i % 2) * 8192 + c * 128:(i % 2) * 8192 + (c + 1) * 128],
                                           HBUF[:, c * 512:(c + 1) * 512], start=(c == 0), stop=(c == 63))
                        sDNM.inc(e, ins, 1)
                        if o >= 2:
                            down_tr(t, e, i - 2)
                    down_tr(t, e, O(t2, 14))
                    down_tr(t, e, O(t2, 15))

            @blk.scalar
            def _(s):
                e = "scalar"
                PB.waitv(e, s, PBASE)
                for t2 in range(2):
                    for c in range(16):
                        i = t2 * 16 + c
                        sSA1.waitv(e, s, i + 1)
                        ins = s.copy(V1[:, c * 512:(c + 1) * 512], pSA[:, (c % 2) * 512:(c % 2 + 1) * 512])
                        if c < 15:
                            sSA1E.inc(e, ins, 1)
                        else:
                            sSA1E.inc(e, s.copy(DUMM[:, :], V_b[:, 0:64]), 1)
                    for j in range(64):
                        i = J(t2, j)
                        sUPM.waitv(e, s, i + 1)
                        sHMU.waitv(e, s, i - 1)
                        ins = s.activation(SG[:, (j % 2) * 512:(j % 2 + 1) * 512],
                                           pGA[:, (j % 2) * 512:(j % 2 + 1) * 512], AF.Silu,
                                           scale=V_b[:, C_SGATE + j:C_SGATE + j + 1])
                        sSIL.inc(e, ins, 1)
                    for j in range(64):
                        i = J(t2, j)
                        sSA2.waitv(e, s, i + 1)
                        ins = s.copy(HBUF[:, j * 512:(j + 1) * 512], pSA[:, (j % 2) * 512:(j % 2 + 1) * 512])
                        if j < 63:
                            sSA2E.inc(e, ins, 1)
                        else:
                            sSA2E.inc(e, s.copy(DUMM[:, :], V_b[:, 0:64]), 1)
                    for o in range(16):
                        i = O(t2, o)
                        sDNM.waitv(e, s, i + 1)
                        if i >= 2:
                            sDTR.waitv(e, s, i - 1)
                        ins = s.activation(DF[:, (i % 2) * 512:(i % 2 + 1) * 512],
                                           pDN[:, (i % 2) * 512:(i % 2 + 1) * 512],
                                           AF.Identity, scale=V_b[:, C_SDP + o:C_SDP + o + 1])
                        sDNE.inc(e, ins, 1)

            assert sHMU.n == 128 and sDNM.n == 32 and sACC.n == 32 and sOUT.n == 16 * 8

        rec.check()
        ctx.close()
        return nc
    except Exception:
        raise


def _make_masks(qblocks):
    m = np.zeros((40, 128, 128), np.float32)
    k = 0
    tri = np.tril(np.ones((128, 128), np.float32)).T
    for kb in range(16):
        for s in range(MSK_LO[kb], N_KB[kb]):
            qb, L = qblocks[s], SLOT_L[s]
            if kb < qb:
                m[k] = 1.0
            elif kb == qb:
                m[k] = tri
            k += 1
    assert k == 40
    return m.astype(bfdt)


_NC_CACHE = {}


def kernel(L0_attn_x, L0_attn_w_qkv, L0_attn_b_qkv, L0_attn_w_out, L0_attn_b_out,
           L0_ffn_SU_ug, L0_ffn_SU_dp, L0_ffn_Wscale_ug, L0_ffn_Wscale_dp,
           L0_ffn_W_up, L0_ffn_W_gate, L0_ffn_W_down):
    x = np.asarray(L0_attn_x, np.float32)
    wqkvT_ = np.asarray(L0_attn_w_qkv, np.float32).T
    woutT_ = np.asarray(L0_attn_w_out, np.float32).T
    wupT_ = np.asarray(L0_ffn_W_up, np.float32).T
    wgateT_ = np.asarray(L0_ffn_W_gate, np.float32).T
    wdownT_ = np.asarray(L0_ffn_W_down, np.float32).T
    shared = {
        "wqkvT": np.ascontiguousarray(
            wqkvT_.reshape(16, 128, 3, 16, 128).transpose(3, 1, 2, 0, 4).reshape(16, 128, 6144)),
        "woutT": np.ascontiguousarray(
            woutT_.reshape(16, 128, 16, 128).transpose(2, 1, 0, 3).reshape(16, 128, 2048)),
        "wupT": np.ascontiguousarray(
            wupT_.reshape(16, 128, 64, 128).transpose(2, 1, 0, 3).reshape(64, 128, 2048)),
        "wgateT": np.ascontiguousarray(
            wgateT_.reshape(16, 128, 64, 128).transpose(2, 1, 0, 3).reshape(64, 128, 2048)),
        "wdownT": np.ascontiguousarray(
            wdownT_.reshape(64, 128, 16, 4, 32).transpose(2, 3, 1, 0, 4).reshape(64, 128, 2048)),
        "bqkv_pc": np.ascontiguousarray(np.asarray(L0_attn_b_qkv, np.float32).reshape(48, 128).T),
        "bout_pc": np.ascontiguousarray(np.asarray(L0_attn_b_out, np.float32).reshape(16, 128).T),
        "suug_pc": np.ascontiguousarray(np.asarray(L0_ffn_SU_ug, np.float32).reshape(16, 128).T),
        "sudp_pc": np.ascontiguousarray(np.asarray(L0_ffn_SU_dp, np.float32).reshape(64, 128).T),
        "wsup_pc": np.ascontiguousarray(np.asarray(L0_ffn_Wscale_ug, np.float32)[:8192].reshape(64, 128).T),
        "wsgate_pc": np.ascontiguousarray(np.asarray(L0_ffn_Wscale_ug, np.float32)[8192:].reshape(64, 128).T),
        "wsdp_pc": np.ascontiguousarray(np.asarray(L0_ffn_Wscale_dp, np.float32).reshape(16, 128).T),
        "h128c": hadamard(128).astype(bfdt),
        "identf": np.eye(128, dtype=np.float32),
        "identb": np.eye(128).astype(bfdt),
        "onescol": np.ones((128, 1), bfdt),
        "negrow": np.full((1, 128), -1.0, np.float32),
        "epsc": np.full((128, 1), 1e-5, np.float32),
    }
    masks = {0: _make_masks(QB_EVEN).reshape(40 * 128, 128),
             1: _make_masks(QB_ODD).reshape(40 * 128, 128)}
    rows = {}
    in_maps = []
    for core in range(8):
        b, half = core // 2, core % 2
        qbs = QB_EVEN if half == 0 else QB_ODD
        r = np.concatenate([np.arange(qb * 128, (qb + 1) * 128) for qb in qbs])
        rows[core] = r
        m = dict(shared)
        m["x_full"] = np.ascontiguousarray(x[b])
        m["x_own"] = np.ascontiguousarray(x[b][r])
        m["maskst"] = masks[half]
        in_maps.append(m)

    import os
    dbg = int(os.environ.get("KERN_DEBUG", "0"))
    if "nc" not in _NC_CACHE:
        _NC_CACHE["nc"] = build_nc(debug=dbg)
    nc = _NC_CACHE["nc"]
    res = run_bass_kernel_spmd(nc, in_maps, core_ids=list(range(8)))
    outp = np.zeros((4, S, D), np.float32)
    for core in range(8):
        outp[core // 2][rows[core]] = np.asarray(res.results[core]["out"])
    if dbg:
        kernel.debug_results = res.results
        kernel.debug_rows = rows
    return outp


if __name__ == "__main__":
    nc = build_nc()
    print("built OK, instructions:", nc.n_instructions())
